# version 03 (final)
# baseline (speedup 1.0000x reference)
import contextlib
import numpy as np
import ml_dtypes
import concourse.bass as bass
import concourse.mybir as mybir
from concourse.bass_utils import run_bass_kernel_spmd

F32 = mybir.dt.float32
BF16 = mybir.dt.bfloat16
I32 = mybir.dt.int32
AF = mybir.ActivationFunctionType
ALU = mybir.AluOpType

NCORES = 8
TOK = 2048
DM = 1024
BF = ml_dtypes.bfloat16
LN_EPS = 1e-5
DEBUG_BARRIERS = False


class Prog:
    ENGINES = ("tensor", "vector", "scalar", "gpsimd", "sync")

    def __init__(self, nc, ctx):
        self.nc = nc
        self.ctx = ctx
        self.ops = {e: [] for e in self.ENGINES}
        self.sems = {}
        self.esem = {}
        for e in self.ENGINES:
            self.esem[e] = "S_" + e
            self.sem("S_" + e)
        self._regions = {}
        self._pending = {e: [] for e in self.ENGINES}
        self._satisfied = {e: {} for e in self.ENGINES}
        self._sbuf_bytes = 0
        self.nwaits = 0

    def sem(self, name):
        if name not in self.sems:
            h = self.ctx.enter_context(self.nc.semaphore(name))
            self.sems[name] = [h, 0]
        return self.sems[name][0]

    def sbuf(self, name, shape, dtype, side=None):
        esz = 2 if dtype in (BF16, mybir.dt.float16) else 4
        self._sbuf_bytes += int(np.prod(shape[1:])) * esz
        if side:
            return self.ctx.enter_context(self.nc.sbuf_tensor(name, shape, dtype, side=side))
        return self.ctx.enter_context(self.nc.sbuf_tensor(name, shape, dtype))

    def psum(self, name, shape, dtype=F32):
        if not hasattr(self, "_psum_names"):
            self._psum_names = set()
        self._psum_names.add(name)
        return self.ctx.enter_context(self.nc.psum_tensor(name, shape, dtype))

    def bump(self, name, k=1):
        s = self.sems[name]
        s[1] += k
        return s[0], s[1]

    def val(self, name):
        return self.sems[name][1]

    def wait(self, engine, name, value):
        if value <= 0:
            return
        sat = self._satisfied[engine]
        if sat.get(name, 0) >= value:
            return
        sat[name] = value
        h = self.sems[name][0]
        self.nwaits += 1
        self.ops[engine].append(lambda e, h=h, v=value: e.wait_ge(h, v))

    def _regs(self, name):
        return self._regions.setdefault(name, [])

    def _resolve_pending(self, engine):
        if not self._pending[engine]:
            return
        h, v = self.bump(self.esem[engine])
        if engine == "tensor":
            self.ops[engine].append(lambda e, h=h: e.drain().then_inc(h, 1))
        else:
            self.ops[engine].append(lambda e, h=h: e.nop().then_inc(h, 1))
        for rec in self._pending[engine]:
            rec[:] = ["SEM", self.esem[engine], v]
        self._pending[engine] = []

    def do(self, engine, fn, reads=(), writes=(), dma=False, defer=False, dsem=None,
           async1=False):
        pn = getattr(self, "_psum_names", ())
        promoted = [rng for rng in reads if rng[0] in pn]
        if promoted:
            writes = list(writes) + promoted
            reads = [rng for rng in reads if rng[0] not in pn]
        """Record op with dependency tracking.

        Milestones: compute ops -> ("SEM", engine_sem, value): wait that value.
        dma/collective ops (async completion, possibly out of order) ->
        ("ASYNC", sem): consumers wait the sem's TOTAL issued count at their
        own record time (guarantees all earlier ops on that sem completed).
        dsem: semaphore name for a dma/async op (default per-destination-buffer).
        """
        deps = []
        for (nm, lo, hi) in reads:
            for reg in self._regs(nm):
                if reg[0] < hi and lo < reg[1] and reg[2] is not None:
                    deps.append(reg[2])
        for (nm, lo, hi) in writes:
            for reg in self._regs(nm):
                if reg[0] < hi and lo < reg[1]:
                    if reg[2] is not None:
                        deps.append(reg[2])
                    deps.extend(reg[3])
        for dep in deps:
            if dep[0] == "PEND":
                self._resolve_pending(dep[1])
        best = {}
        for dep in deps:
            if dep[0] == "ASYNC":
                nm = dep[1]
                best[nm] = max(best.get(nm, 0), self.val(nm))
            else:
                _, snm, dval = dep
                best[snm] = max(best.get(snm, 0), dval)
        own = self.esem[engine]
        own_next = self.val(own) + 1
        for snm, dval in sorted(best.items()):
            if snm == own and dval >= own_next:
                continue
            self.wait(engine, snm, dval)
        if defer or (engine == "tensor" and not (dma or async1)):
            mil = ["PEND", engine, None]
            self._pending[engine].append(mil)
            self.ops[engine].append(fn)
        elif dma or async1:
            if dsem is None:
                dsem = "D_%s_%d" % (writes[0][0], writes[0][1]) if writes else "D_misc"
            self.sem(dsem)
            kk = 16 if dma else 1
            h, v = self.bump(dsem, kk)
            mil = ("ASYNC", dsem)
            self.ops[engine].append(lambda e, f=fn, h=h, kk=kk: f(e).then_inc(h, kk))
        else:
            h, v = self.bump(own, 1)
            mil = ("SEM", own, v)
            self.ops[engine].append(lambda e, f=fn, h=h: f(e).then_inc(h, 1))
        for (nm, lo, hi) in writes:
            regs = self._regs(nm)
            newregs = []
            for reg in regs:
                if reg[0] < hi and lo < reg[1]:
                    if reg[0] < lo:
                        newregs.append([reg[0], lo, reg[2], list(reg[3])])
                    if hi < reg[1]:
                        newregs.append([hi, reg[1], reg[2], list(reg[3])])
                else:
                    newregs.append(reg)
            newregs.append([lo, hi, mil, []])
            self._regions[nm] = newregs
        for (nm, lo, hi) in reads:
            regs = self._regs(nm)
            hit = False
            for reg in regs:
                if reg[0] < hi and lo < reg[1]:
                    reg[3].append(mil)
                    hit = True
            if not hit:
                regs.append([lo, hi, None, [mil]])
        return mil

    def milestone(self, engine):
        self._resolve_pending(engine)
        return ("SEM", self.esem[engine], self.val(self.esem[engine]))

    def barrier_all(self):
        for e in self.ENGINES:
            self._resolve_pending(e)
        self.sem("S_bar")
        for e in self.ENGINES:
            h, v = self.bump("S_bar", 1)
            self.ops[e].append(lambda en, h=h: en.nop().then_inc(h, 1))
        tgt = self.val("S_bar")
        for e in self.ENGINES:
            self.wait(e, "S_bar", tgt)

    def op(self, engine, fn):
        self.ops[engine].append(fn)

    def emit(self):
        block = self.ctx.enter_context(self.nc.Block())
        for eng in self.ENGINES:
            lst = self.ops[eng]
            if not lst:
                continue
            def mk(lst):
                def run(e):
                    for fn in lst:
                        fn(e)
                return run
            getattr(block, eng)(mk(lst))


def rr(h, lo, hi):
    return (h.name, lo, hi)


def rrf(h):
    return (h.name, 0, 1 << 62)


def blob_mat(blob_ap_flat, off, nrows, ncols):
    return blob_ap_flat[off:off + nrows * ncols].rearrange("(r c) -> r c", c=ncols)



def emit_ident(P, ident):
    itmp = P.sbuf("id_itmp", [128, 128], I32)
    P.do("gpsimd", lambda e: e.iota(itmp[:, :], [[1, 128]], base=0, channel_multiplier=-1),
         writes=[rrf(itmp)])
    P.do("vector", lambda e: e.tensor_scalar(ident[:, :], itmp[:, :], 0.0, None, ALU.is_equal),
         reads=[rrf(itmp)], writes=[rrf(ident)])


def emit_load_res(P, x_ap, res, ident, banks, n_tok=TOK, name="ldr", res_col0=0, res_ntok=TOK,
                  xt=None):
    nt = n_tok // 128
    if xt is None:
        xt = P.sbuf(name + "_xt", [128, 2 * DM], F32)
    xnm = x_ap.tensor.name
    for i in range(nt):
        buf = i % 2
        P.do("sync", lambda e, buf=buf, i=i: e.dma_start(
                 out=xt[:, buf * DM:(buf + 1) * DM], in_=x_ap[i * 128:(i + 1) * 128, :]),
             reads=[(xnm, i * 128 * DM, (i + 1) * 128 * DM)],
             writes=[rr(xt, buf * DM, (buf + 1) * DM)], dma=True)
        for c in range(8):
            bank = banks[(i % 2) * 2 + (c // 4)]
            P.do("tensor", lambda e, b=bank, c=c, buf=buf: e.transpose(
                     b[:, (c % 4) * 128:(c % 4) * 128 + 128],
                     xt[:, buf * DM + c * 128: buf * DM + (c + 1) * 128], ident[:, :]),
                 reads=[rr(xt, buf * DM + c * 128, buf * DM + (c + 1) * 128), rrf(ident)],
                 writes=[rr(bank, (c % 4) * 128, (c % 4) * 128 + 128)])
        for half in range(2):
            bank = banks[(i % 2) * 2 + half]
            src = bank[:, :].rearrange("p (c t) -> p c t", c=4)
            dst = res[:, :].rearrange("p (c t) -> p c t", t=res_ntok)[:, half * 4:half * 4 + 4,
                                                                     res_col0 + i * 128:res_col0 + (i + 1) * 128]
            P.do("vector", lambda e, d=dst, s=src: e.tensor_copy(d, s),
                 reads=[rr(bank, 0, 512)],
                 writes=[(res.name, (half * 4) * res_ntok + res_col0 + i * 128,
                          (half * 4 + 3) * res_ntok + res_col0 + (i + 1) * 128)])


def emit_store_res(P, y_ap, res, ident, banks, n_tok=TOK, name="str", yt=None):
    nt = n_tok // 128
    if yt is None:
        yt = P.sbuf(name + "_yt", [128, 2 * DM], F32)
    ynm = y_ap.tensor.name
    for i in range(nt):
        for c in range(8):
            bank = banks[(i % 2) * 2 + (c // 4)]
            P.do("tensor", lambda e, b=bank, c=c, i=i: e.transpose(
                     b[:, (c % 4) * 128:(c % 4) * 128 + 128],
                     res[:, c * n_tok + i * 128: c * n_tok + (i + 1) * 128], ident[:, :]),
                 reads=[rr(res, c * n_tok + i * 128, c * n_tok + (i + 1) * 128), rrf(ident)],
                 writes=[rr(bank, (c % 4) * 128, (c % 4) * 128 + 128)])
        buf = i % 2
        for half in range(2):
            bank = banks[(i % 2) * 2 + half]
            P.do("vector", lambda e, d=yt[:, buf * DM + half * 512: buf * DM + (half + 1) * 512],
                 s=bank[:, :]: e.tensor_copy(d, s),
                 reads=[rr(bank, 0, 512)],
                 writes=[rr(yt, buf * DM + half * 512, buf * DM + (half + 1) * 512)])
        P.do("sync", lambda e, buf=buf, i=i: e.dma_start(
                 out=y_ap[i * 128:(i + 1) * 128, :], in_=yt[:, buf * DM:(buf + 1) * DM]),
             reads=[rr(yt, buf * DM, (buf + 1) * DM)],
             writes=[(ynm, i * 128 * DM, (i + 1) * 128 * DM)], dma=True, dsem="D_st_" + ynm)
    P.wait("sync", "D_st_" + ynm, P.val("D_st_" + ynm))


def emit_weight_gather(P, wshard_ap, shard_elems, blob, name="wg"):
    nc = P.nc
    sh_bf = nc.dram_tensor(name + "_shbf", [1, shard_elems], BF16)
    wnm = wshard_ap.tensor.name
    CH = 1 << 21
    for off in range(0, shard_elems, CH):
        sz = min(CH, shard_elems - off)
        P.do("gpsimd", lambda e, off=off, sz=sz: e.dma_start(
                 out=sh_bf.ap()[0:1, off:off + sz], in_=wshard_ap[0:1, off:off + sz]),
             reads=[(wnm, off, off + sz)],
             writes=[(sh_bf.name, off, off + sz)], dma=True, dsem="D_" + name + "_cast")
    P.do("gpsimd", lambda e: e.collective_compute(
             "AllGather", ALU.bypass,
             replica_groups=[list(range(NCORES))],
             ins=[sh_bf.ap().opt()],
             outs=[blob.ap().opt()]),
         reads=[(sh_bf.name, 0, shard_elems)],
         writes=[(blob.name, 0, NCORES * shard_elems)],
         async1=True, dsem="C_" + name)


class LNArea:
    def __init__(self, P, tag="lna"):
        self.xb = P.sbuf(tag + "_xb", [128, 8 * 512], BF16)
        self.sq = P.sbuf(tag + "_sq", [128, 8 * 512], BF16)
        self.tmp = P.sbuf(tag + "_tmp", [128, 512], F32)
        self.mu_row = P.sbuf(tag + "_mu", [1, 512], F32)
        self.t_row = P.sbuf(tag + "_t", [1, 512], F32)
        self.rstd_row = P.sbuf(tag + "_r", [1, 512], F32)
        self.mu_b = P.sbuf(tag + "_mub", [128, 512], F32)
        self.r_b = P.sbuf(tag + "_rb", [128, 512], F32)
        self.ones_bf = P.sbuf(tag + "_o1", [128, 1], BF16)
        self.ones_f = P.sbuf(tag + "_o2", [1, 128], F32)
        self.epscol = P.sbuf(tag + "_eps", [1, 1], F32)
        P.do("vector", lambda e: e.memset(self.ones_bf[:, :], 1.0), writes=[rrf(self.ones_bf)])
        P.do("vector", lambda e: e.memset(self.epscol[:, :], LN_EPS), writes=[rrf(self.epscol)])
        P.do("vector", lambda e: e.memset(self.ones_f[:, :], 1.0), writes=[rrf(self.ones_f)])


def emit_ln(P, A, src_fn, src_rng, xn, banks, gamma_col=None, beta_col=None):
    TT = 512
    pb_s1, pb_s2, pb_b = banks
    for c in range(8):
        src, srng = src_fn(c), src_rng(c)
        P.do("vector", lambda e, c=c, s=src: e.tensor_copy(A.xb[:, c * TT:(c + 1) * TT], s),
             reads=[srng], writes=[rr(A.xb, c * TT, (c + 1) * TT)])
        P.do("scalar", lambda e, c=c, s=src: e.activation(A.sq[:, c * TT:(c + 1) * TT], s, AF.Square),
             reads=[srng], writes=[rr(A.sq, c * TT, (c + 1) * TT)])
    for c in range(8):
        P.do("tensor", lambda e, c=c: e.matmul(pb_s1[0:1, :], A.ones_bf[:, 0:1],
                                               A.xb[:, c * TT:(c + 1) * TT],
                                               start=(c == 0), stop=(c == 7)),
             reads=[rr(A.xb, c * TT, (c + 1) * TT), rrf(A.ones_bf)],
             writes=[rr(pb_s1, 0, 512)], defer=(c != 7))
    for c in range(8):
        P.do("tensor", lambda e, c=c: e.matmul(pb_s2[0:1, :], A.ones_bf[:, 0:1],
                                               A.sq[:, c * TT:(c + 1) * TT],
                                               start=(c == 0), stop=(c == 7)),
             reads=[rr(A.sq, c * TT, (c + 1) * TT), rrf(A.ones_bf)],
             writes=[rr(pb_s2, 0, 512)], defer=(c != 7))
    P.do("scalar", lambda e: e.activation(A.mu_row[0:1, :], pb_s1[0:1, :], AF.Copy, scale=1.0 / DM),
         reads=[rr(pb_s1, 0, 512)], writes=[rrf(A.mu_row)])
    P.do("scalar", lambda e: e.activation(A.t_row[0:1, :], pb_s2[0:1, :], AF.Copy, scale=1.0 / DM),
         reads=[rr(pb_s2, 0, 512)], writes=[rrf(A.t_row)])
    P.do("vector", lambda e: e.tensor_tensor(A.rstd_row[0:1, :], A.mu_row[0:1, :],
                                             A.mu_row[0:1, :], ALU.mult),
         reads=[rrf(A.mu_row)], writes=[rrf(A.rstd_row)])
    P.do("vector", lambda e: e.tensor_tensor(A.rstd_row[0:1, :], A.t_row[0:1, :],
                                             A.rstd_row[0:1, :], ALU.subtract),
         reads=[rrf(A.t_row), rrf(A.rstd_row)], writes=[rrf(A.rstd_row)])
    P.do("scalar", lambda e: e.activation(A.t_row[0:1, :], A.rstd_row[0:1, :], AF.Ln,
                                          bias=A.epscol[0:1, 0:1]),
         reads=[rrf(A.rstd_row), rrf(A.epscol)], writes=[rrf(A.t_row)])
    P.do("scalar", lambda e: e.activation(A.rstd_row[0:1, :], A.t_row[0:1, :], AF.Exp, scale=-0.5),
         reads=[rrf(A.t_row)], writes=[rrf(A.rstd_row)])
    P.do("tensor", lambda e: e.matmul(pb_b[:, :], A.ones_f[0:1, :], A.mu_row[0:1, :],
                                      start=True, stop=True),
         reads=[rrf(A.mu_row), rrf(A.ones_f)], writes=[rr(pb_b, 0, 512)])
    P.do("vector", lambda e: e.tensor_copy(A.mu_b[:, :], pb_b[:, :]),
         reads=[rr(pb_b, 0, 512)], writes=[rrf(A.mu_b)])
    P.do("tensor", lambda e: e.matmul(pb_b[:, :], A.ones_f[0:1, :], A.rstd_row[0:1, :],
                                      start=True, stop=True),
         reads=[rrf(A.rstd_row), rrf(A.ones_f)], writes=[rr(pb_b, 0, 512)])
    P.do("vector", lambda e: e.tensor_copy(A.r_b[:, :], pb_b[:, :]),
         reads=[rr(pb_b, 0, 512)], writes=[rrf(A.r_b)])
    for c in range(8):
        src, srng = src_fn(c), src_rng(c)
        P.do("vector", lambda e, s=src: e.tensor_tensor(A.tmp[:, :], s, A.mu_b[:, :], ALU.subtract),
             reads=[srng, rrf(A.mu_b)], writes=[rrf(A.tmp)])
        if gamma_col is None:
            P.do("vector", lambda e, c=c: e.tensor_tensor(xn[:, c * TT:(c + 1) * TT],
                                                          A.tmp[:, :], A.r_b[:, :], ALU.mult),
                 reads=[rrf(A.tmp), rrf(A.r_b)], writes=[rr(xn, c * TT, (c + 1) * TT)])
        else:
            gap, grng = gamma_col(c)
            bap, brng = beta_col(c)
            P.do("vector", lambda e, g=gap: e.scalar_tensor_tensor(A.tmp[:, :], A.tmp[:, :], g,
                                                                   A.r_b[:, :], ALU.mult, ALU.mult),
                 reads=[rrf(A.tmp), grng, rrf(A.r_b)], writes=[rrf(A.tmp)])
            P.do("vector", lambda e, c=c, b=bap: e.tensor_scalar(xn[:, c * TT:(c + 1) * TT],
                                                                 A.tmp[:, :], b, None, ALU.add),
                 reads=[rrf(A.tmp), brng], writes=[rr(xn, c * TT, (c + 1) * TT)])


def emit_mm_stage(P, wblob, blob_name, woff, wrows, n_in_ch, m_groups, xn_src, out_fn,
                  banks, wbuf, mg_width=4):
    W = blob_mat(wblob, woff, wrows, n_in_ch * 128)
    n_dma = 0
    for mg in range(m_groups):
        m0 = mg * mg_width * 128
        mw = min(mg_width * 128, wrows - m0)
        nj = (mw + 127) // 128
        for k in range(n_in_ch):
            slot = n_dma % 2
            scol = slot * mg_width * 128
            P.do("sync", lambda e, scol=scol, m0=m0, mw=mw, k=k: e.dma_start(
                     out=wbuf[:, scol:scol + mw],
                     in_=W[m0:m0 + mw, k * 128:(k + 1) * 128], transpose=True),
                 reads=[(blob_name, woff + m0 * n_in_ch * 128, woff + (m0 + mw) * n_in_ch * 128)],
                 writes=[rr(wbuf, scol, scol + mw)], dma=True)
            xap, xrng = xn_src(k)
            for j in range(nj):
                jw = min(128, mw - j * 128)
                bank = banks[(mg % 2) * 4 + j]
                P.do("tensor", lambda e, b=bank, scol=scol, j=j, jw=jw, k=k, xap=xap: e.matmul(
                         b[0:jw, :], wbuf[:, scol + j * 128: scol + j * 128 + jw], xap,
                         start=(k == 0), stop=(k == n_in_ch - 1)),
                     reads=[rr(wbuf, scol + j * 128, scol + j * 128 + jw), xrng],
                     writes=[rr(bank, 0, 512)], defer=(k != n_in_ch - 1))
            n_dma += 1
        for j in range(nj):
            jw = min(128, mw - j * 128)
            out_fn(mg, j, jw, banks[(mg % 2) * 4 + j])



L0FFN_SIZES = [(4096, 1024), (4096, 1024), (4096, 4096), (4096, 4096), (1024, 4096)]
L0FFN_OFFS = []
_o = 0
for _r, _c in L0FFN_SIZES:
    L0FFN_OFFS.append(_o)
    _o += _r * _c
L0FFN_TOT = _o
L0FFN_SH = L0FFN_TOT // NCORES


def emit_scales_load(P, sc_ap, nrows, scols, scraw, ident, banks):
    scnm = sc_ap.tensor.name
    rem = nrows - 128
    P.do("sync", lambda e: e.dma_start(out=scraw[0:128, 0:128], in_=sc_ap[0:128, :]),
         reads=[(scnm, 0, 128 * 128)], writes=[rr(scraw, 0, 128)], dma=True)
    P.do("sync", lambda e: e.dma_start(out=scraw[0:rem, 128:256], in_=sc_ap[128:nrows, :]),
         reads=[(scnm, 128 * 128, nrows * 128)],
         writes=[rr(scraw, 128, 256)], dma=True)
    P.do("tensor", lambda e: e.transpose(banks[0][:, 0:128], scraw[0:128, 0:128], ident[:, :]),
         reads=[rr(scraw, 0, 128), rrf(ident)], writes=[rr(banks[0], 0, 128)])
    P.do("tensor", lambda e: e.transpose(banks[1][:, 0:rem], scraw[0:rem, 128:256],
                                         ident[0:rem, 0:rem]),
         reads=[rr(scraw, 128, 256), rrf(ident)], writes=[rr(banks[1], 0, rem)])
    P.do("vector", lambda e: e.tensor_copy(scols[:, 0:128], banks[0][:, 0:128]),
         reads=[rr(banks[0], 0, 128)], writes=[rr(scols, 0, 128)])
    P.do("vector", lambda e: e.tensor_copy(scols[:, 128:nrows], banks[1][:, 0:rem]),
         reads=[rr(banks[1], 0, rem)], writes=[rr(scols, 128, nrows)])


def emit_l0ffn_core(P, blobf, blob_name, res, xn, hA, hB, ss, scols, banks, wbufu, wbufg, A):
    W1 = blob_mat(blobf, L0FFN_OFFS[0], 4096, 1024)
    W2 = blob_mat(blobf, L0FFN_OFFS[1], 4096, 1024)
    for t in range(4):
        toff = t * 512
        if DEBUG_BARRIERS:
            P.barrier_all()
        emit_ln(P, A,
                lambda c, toff=toff: res[:, c * TOK + toff: c * TOK + toff + 512],
                lambda c, toff=toff: (res.name, c * TOK + toff, c * TOK + toff + 512),
                xn, banks[5:8])

        n_dma = 0
        for mg in range(8):
            m0 = mg * 512
            for k in range(8):
                slot = n_dma % 2
                P.do("sync", lambda e, slot=slot, m0=m0, k=k: e.dma_start(
                         out=wbufu[:, slot * 512:slot * 512 + 512],
                         in_=W1[m0:m0 + 512, k * 128:(k + 1) * 128], transpose=True),
                     reads=[(blob_name, L0FFN_OFFS[0] + m0 * 1024, L0FFN_OFFS[0] + (m0 + 512) * 1024)],
                     writes=[rr(wbufu, slot * 512, slot * 512 + 512)], dma=True)
                P.do("sync", lambda e, slot=slot, m0=m0, k=k: e.dma_start(
                         out=wbufu[:, 1024 + slot * 512:1024 + slot * 512 + 512],
                         in_=W2[m0:m0 + 512, k * 128:(k + 1) * 128], transpose=True),
                     reads=[(blob_name, L0FFN_OFFS[1] + m0 * 1024, L0FFN_OFFS[1] + (m0 + 512) * 1024)],
                     writes=[rr(wbufu, 1024 + slot * 512, 1024 + slot * 512 + 512)], dma=True)
                for j in range(4):
                    P.do("tensor", lambda e, b=banks[j], slot=slot, j=j, k=k: e.matmul(
                             b[:, :], wbufu[:, slot * 512 + j * 128: slot * 512 + j * 128 + 128],
                             xn[:, k * 512:(k + 1) * 512], start=(k == 0), stop=(k == 7)),
                         reads=[rr(wbufu, slot * 512 + j * 128, slot * 512 + j * 128 + 128),
                                rr(xn, k * 512, (k + 1) * 512)],
                         writes=[rr(banks[j], 0, 512)], defer=(k != 7))
                    P.do("tensor", lambda e, b=banks[4 + j], slot=slot, j=j, k=k: e.matmul(
                             b[:, :], wbufu[:, 1024 + slot * 512 + j * 128: 1024 + slot * 512 + j * 128 + 128],
                             xn[:, k * 512:(k + 1) * 512], start=(k == 0), stop=(k == 7)),
                         reads=[rr(wbufu, 1024 + slot * 512 + j * 128, 1024 + slot * 512 + j * 128 + 128),
                                rr(xn, k * 512, (k + 1) * 512)],
                         writes=[rr(banks[4 + j], 0, 512)], defer=(k != 7))
                n_dma += 1
            for j in range(4):
                m = mg * 4 + j
                P.do("scalar", lambda e, j=j, m=m: e.activation(ss[:, j * 512:(j + 1) * 512],
                                                                banks[j][:, :], AF.Silu,
                                                                scale=scols[:, m:m + 1]),
                     reads=[rr(banks[j], 0, 512), rr(scols, m, m + 1)],
                     writes=[rr(ss, j * 512, (j + 1) * 512)])
                P.do("vector", lambda e, j=j, m=m: e.scalar_tensor_tensor(
                         hA[:, m * 512:(m + 1) * 512], banks[4 + j][:, :],
                         scols[:, 32 + m:32 + m + 1], ss[:, j * 512:(j + 1) * 512],
                         ALU.mult, ALU.mult),
                     reads=[rr(banks[4 + j], 0, 512), rr(scols, 32 + m, 33 + m),
                            rr(ss, j * 512, (j + 1) * 512)],
                     writes=[rr(hA, m * 512, (m + 1) * 512)])

        def mk_out(dst, scol_base):
            def out_fn(mg, j, jw, bank):
                m = mg * 4 + j
                P.do("scalar", lambda e, b=bank, m=m, jw=jw: e.activation(
                         dst[0:jw, m * 512:(m + 1) * 512], b[0:jw, :], AF.Gelu,
                         scale=scols[0:jw, scol_base + m:scol_base + m + 1]),
                     reads=[rr(bank, 0, 512), rr(scols, scol_base + m, scol_base + m + 1)],
                     writes=[rr(dst, m * 512, (m + 1) * 512)])
            return out_fn

        emit_mm_stage(P, blobf, blob_name, L0FFN_OFFS[2], 4096, 32, 8,
                      lambda k: (hA[:, k * 512:(k + 1) * 512], rr(hA, k * 512, (k + 1) * 512)),
                      mk_out(hB, 64), banks, wbufg)
        emit_mm_stage(P, blobf, blob_name, L0FFN_OFFS[3], 4096, 32, 8,
                      lambda k: (hB[:, k * 512:(k + 1) * 512], rr(hB, k * 512, (k + 1) * 512)),
                      mk_out(hA, 96), banks, wbufg)

        def w3_out(mg, j, jw, bank, toff=toff):
            m = mg * 4 + j
            P.do("vector", lambda e, b=bank, m=m: e.scalar_tensor_tensor(
                     res[:, m * TOK + toff: m * TOK + toff + 512], b[:, :],
                     scols[:, 128 + m:128 + m + 1],
                     res[:, m * TOK + toff: m * TOK + toff + 512],
                     ALU.mult, ALU.add),
                 reads=[rr(bank, 0, 512), rr(scols, 128 + m, 129 + m),
                        rr(res, m * TOK + toff, m * TOK + toff + 512)],
                 writes=[rr(res, m * TOK + toff, m * TOK + toff + 512)])

        emit_mm_stage(P, blobf, blob_name, L0FFN_OFFS[4], 1024, 32, 2,
                      lambda k: (hA[:, k * 512:(k + 1) * 512], rr(hA, k * 512, (k + 1) * 512)),
                      w3_out, banks, wbufg)


def build_l0_ffn():
    nc = bass.Bass("TRN2", target_bir_lowering=False)
    x_ap = nc.dram_tensor("x", [TOK, DM], F32, kind="ExternalInput").ap()
    wsh_ap = nc.dram_tensor("wsh", [1, L0FFN_SH], F32, kind="ExternalInput").ap()
    sc_ap = nc.dram_tensor("scales", [136, 128], F32, kind="ExternalInput").ap()
    y_ap = nc.dram_tensor("y", [TOK, DM], F32, kind="ExternalOutput").ap()
    blob = nc.dram_tensor("blob", [NCORES, L0FFN_SH], BF16)
    blobf = blob.ap().rearrange("r c -> (r c)")

    with contextlib.ExitStack() as ctx:
        P = Prog(nc, ctx)
        banks = [P.psum(f"pb{i}", [128, 512], F32) for i in range(8)]
        res = P.sbuf("res", [128, 8 * TOK], F32)
        xn = P.sbuf("xn", [128, 8 * 512], BF16)
        hA = P.sbuf("hA", [128, 32 * 512], BF16)
        hB = P.sbuf("hB", [128, 32 * 512], BF16)
        ss = P.sbuf("ss", [128, 4 * 512], F32)
        ident = P.sbuf("ident", [128, 128], F32)
        scols = P.sbuf("scols", [128, 136], F32)
        scraw = P.sbuf("scraw", [128, 256], F32)
        wbufu = P.sbuf("wbu", [128, 4 * 512], BF16)
        wbufg = P.sbuf("wbg", [128, 2 * 512], BF16)
        A = LNArea(P)

        emit_ident(P, ident)
        emit_weight_gather(P, wsh_ap, L0FFN_SH, blob)
        emit_scales_load(P, sc_ap, 136, scols, scraw, ident, [banks[4], banks[5]])
        P.do("vector", lambda e: e.tensor_scalar(scols[:, 128:136], scols[:, 128:136],
                                                 0.5, None, ALU.mult),
             reads=[rr(scols, 128, 136)], writes=[rr(scols, 128, 136)])
        emit_load_res(P, x_ap, res, ident, banks[0:4])
        emit_l0ffn_core(P, blobf, blob.name, res, xn, hA, hB, ss, scols, banks, wbufu, wbufg, A)
        emit_store_res(P, y_ap, res, ident, banks[0:4])
        P.emit()
    return nc



_PROG_CACHE = {}


def _l0ffn_pack(inputs):
    blob = np.concatenate([
        inputs["L0_ffn_w1_w"].ravel(), inputs["L0_ffn_w2_w"].ravel(),
        inputs["L0_ffn_h1_w"].ravel(), inputs["L0_ffn_h2_w"].ravel(),
        inputs["L0_ffn_w3_w"].ravel()]).astype(np.float32, copy=False)
    shards = blob.reshape(NCORES, L0FFN_SH)
    scales = np.concatenate([
        inputs["L0_ffn_w1_s"], inputs["L0_ffn_w2_s"], inputs["L0_ffn_h1_s"],
        inputs["L0_ffn_h2_s"], inputs["L0_ffn_w3_s"]]).astype(np.float32, copy=False)
    scales = scales.reshape(136, 128)
    return shards, scales


def run_l0_ffn(x_full, inputs):
    if "l0ffn" not in _PROG_CACHE:
        _PROG_CACHE["l0ffn"] = build_l0_ffn()
    nc = _PROG_CACHE["l0ffn"]
    shards, scales = _l0ffn_pack(inputs)
    xs = x_full.reshape(NCORES, TOK, DM)
    in_maps = [{"x": np.ascontiguousarray(xs[c]),
                "wsh": shards[c:c + 1],
                "scales": scales} for c in range(NCORES)]
    res = run_bass_kernel_spmd(nc, in_maps, core_ids=list(range(NCORES)))
    y = np.stack([res.results[c]["y"] for c in range(NCORES)], axis=0)
    return y.reshape(2, 8192, DM)


def kernel(**inputs):
    x = np.asarray(inputs["L0_attn_x"], dtype=np.float32)
    x = run_l0_ffn(x, inputs)
    return x



L0A_SLOTS = [16, 32, 48, 64]


def l0a_chunks(core):
    m = core % 4
    return [m, 7 - m, 8 + m, 15 - m]


def build_l0_attn_kv():
    nc = bass.Bass("TRN2", target_bir_lowering=False)
    x_ap = nc.dram_tensor("x", [TOK, DM], F32, kind="ExternalInput").ap()
    wk_ap = nc.dram_tensor("wk", [DM, 64], F32, kind="ExternalInput").ap()
    wv_ap = nc.dram_tensor("wv", [DM, 64], F32, kind="ExternalInput").ap()
    ind_ap = nc.dram_tensor("indkv", [128, 2], F32, kind="ExternalInput").ap()
    klat_ap = nc.dram_tensor("klat", [64, TOK], BF16, kind="ExternalOutput").ap()
    ksq_ap = nc.dram_tensor("ksq", [2, TOK], F32, kind="ExternalOutput").ap()
    vT_ap = nc.dram_tensor("vT", [TOK, 64], BF16, kind="ExternalOutput").ap()

    with contextlib.ExitStack() as ctx:
        P = Prog(nc, ctx)
        banks = [P.psum(f"pb{i}", [128, 512], F32) for i in range(7)]
        pbf = P.psum("pbf", [128, 1024], BF16)
        res = P.sbuf("res", [128, 8 * TOK], F32)
        xn = P.sbuf("xn", [128, 8 * 512], BF16)
        ident = P.sbuf("ident", [128, 128], F32)
        identb = P.sbuf("identb", [128, 128], BF16)
        wkv = P.sbuf("wkv", [128, 8 * 128], BF16)
        indb = P.sbuf("indb", [128, 2], BF16)
        kl = P.sbuf("kl", [128, TOK], BF16)
        sqs = P.sbuf("sqs", [128, 512], BF16)
        ksqs = P.sbuf("ksqs", [2, TOK], F32)
        vts = P.sbuf("vts", [128, 64], BF16)
        A = LNArea(P)

        emit_ident(P, ident)
        P.do("vector", lambda e: e.tensor_copy(identb[:, :], ident[:, :]),
             reads=[rrf(ident)], writes=[rrf(identb)])
        for k in range(8):
            P.do("gpsimd", lambda e, k=k: e.dma_start(
                     out=wkv[:, k * 128:k * 128 + 64], in_=wk_ap[k * 128:(k + 1) * 128, :]),
                 reads=[("wk", k * 128 * 64, (k + 1) * 128 * 64)],
                 writes=[rr(wkv, k * 128, k * 128 + 64)], dma=True, dsem="D_wkv")
            P.do("gpsimd", lambda e, k=k: e.dma_start(
                     out=wkv[:, k * 128 + 64:k * 128 + 128], in_=wv_ap[k * 128:(k + 1) * 128, :]),
                 reads=[("wv", k * 128 * 64, (k + 1) * 128 * 64)],
                 writes=[rr(wkv, k * 128 + 64, k * 128 + 128)], dma=True, dsem="D_wkv")
        P.do("gpsimd", lambda e: e.dma_start(out=indb[:, :], in_=ind_ap),
             reads=[("indkv", 0, 256)], writes=[rrf(indb)], dma=True)
        emit_load_res(P, x_ap, res, ident, banks[0:4])

        for t in range(4):
            toff = t * 512
            emit_ln(P, A,
                    lambda c, toff=toff: res[:, c * TOK + toff: c * TOK + toff + 512],
                    lambda c, toff=toff: (res.name, c * TOK + toff, c * TOK + toff + 512),
                    xn, banks[4:7])
            for k in range(8):
                P.do("tensor", lambda e, k=k: e.matmul(
                         banks[0][:, :], wkv[:, k * 128:(k + 1) * 128],
                         xn[:, k * 512:(k + 1) * 512], start=(k == 0), stop=(k == 7)),
                     reads=[rr(wkv, k * 128, (k + 1) * 128), rr(xn, k * 512, (k + 1) * 512)],
                     writes=[rr(banks[0], 0, 512)], defer=True)
            P.do("vector", lambda e, toff=toff: e.tensor_copy(kl[:, toff:toff + 512], banks[0][:, :]),
                 reads=[rr(banks[0], 0, 512)], writes=[rr(kl, toff, toff + 512)])
            P.do("scalar", lambda e: e.activation(sqs[:, :], banks[0][:, :], AF.Square),
                 reads=[rr(banks[0], 0, 512)], writes=[rrf(sqs)])
            P.do("tensor", lambda e: e.matmul(banks[1][0:2, :], indb[:, :], sqs[:, :],
                                              start=True, stop=True),
                 reads=[rrf(sqs), rrf(indb)], writes=[rr(banks[1], 0, 512)])
            P.do("vector", lambda e, toff=toff: e.tensor_copy(ksqs[:, toff:toff + 512],
                                                              banks[1][0:2, :]),
                 reads=[rr(banks[1], 0, 512)], writes=[rr(ksqs, toff, toff + 512)])
            for b2 in range(4):
                P.do("tensor", lambda e, toff=toff, b2=b2: e.transpose(
                         pbf[:, 0:64],
                         kl[64:128, toff + b2 * 128: toff + (b2 + 1) * 128],
                         identb[64:128, 64:128]),
                     reads=[rr(kl, toff + b2 * 128, toff + (b2 + 1) * 128), rrf(identb)],
                     writes=[rr(pbf, 0, 64)])
                P.do("vector", lambda e: e.tensor_copy(vts[:, :], pbf[:, 0:64]),
                     reads=[rr(pbf, 0, 64)], writes=[rrf(vts)])
                P.do("sync", lambda e, toff=toff, b2=b2: e.dma_start(
                         out=vT_ap[toff + b2 * 128: toff + (b2 + 1) * 128, :], in_=vts[:, :]),
                     reads=[rrf(vts)], writes=[("vT", (toff + b2 * 128) * 64,
                                                (toff + (b2 + 1) * 128) * 64)],
                     dma=True, dsem="D_vT")
        P.do("sync", lambda e: e.dma_start(out=klat_ap, in_=kl[0:64, :]),
             reads=[rr(kl, 0, TOK)], writes=[("klat", 0, 64 * TOK)], dma=True, dsem="D_ko")
        P.do("sync", lambda e: e.dma_start(out=ksq_ap, in_=ksqs[:, :]),
             reads=[rrf(ksqs)], writes=[("ksq", 0, 2 * TOK)], dma=True, dsem="D_ko")
        P.wait("sync", "D_ko", P.val("D_ko"))
        P.wait("sync", "D_vT", P.val("D_vT"))
        P.emit()
    return nc


def build_l0_attn_main():
    nc = bass.Bass("TRN2", target_bir_lowering=False)
    x_ap = nc.dram_tensor("x", [TOK, DM], F32, kind="ExternalInput").ap()
    wq_ap = nc.dram_tensor("wq", [DM, DM], F32, kind="ExternalInput").ap()
    wo_ap = nc.dram_tensor("wo", [DM, DM], F32, kind="ExternalInput").ap()
    wku_ap = nc.dram_tensor("wku", [32, 128], F32, kind="ExternalInput").ap()
    wvu_ap = nc.dram_tensor("wvu", [32, 128], F32, kind="ExternalInput").ap()
    kq_aps, vq_aps = [], []
    for j, M in enumerate(L0A_SLOTS):
        kq_aps.append(nc.dram_tensor(f"kq{j}", [2, 33, M * 128], BF16, kind="ExternalInput").ap())
        vq_aps.append(nc.dram_tensor(f"vq{j}", [2, M * 128, 33], BF16, kind="ExternalInput").ap())
    NGRP = [M // 4 for M in L0A_SLOTS]
    GOFF = [0]
    for M in L0A_SLOTS:
        GOFF.append(GOFF[-1] + M // 4)
    bc_ap = nc.dram_tensor("bcols", [128, GOFF[-1]], F32, kind="ExternalInput").ap()
    dm_ap = nc.dram_tensor("dmask", [128, 4 * 512], BF16, kind="ExternalInput").ap()
    ksq_ap = nc.dram_tensor("ksqf", [1, 2 * 8192], F32, kind="ExternalInput").ap()
    y_ap = nc.dram_tensor("y", [TOK, DM], F32, kind="ExternalOutput").ap()
    MMAX = max(L0A_SLOTS)

    with contextlib.ExitStack() as ctx:
        P = Prog(nc, ctx)
        pbigA = P.psum("pbigA", [128, 2048], F32)
        pvb = P.psum("pvb", [128, 512], F32)
        pbx = P.psum("pbx", [128, 512], F32)
        pwo = P.psum("pwo", [128, 1024], F32)
        ldbanks = [pbigA[:, i * 512:(i + 1) * 512] for i in range(4)]
        lnbanks = [pbigA[:, 0:512], pbigA[:, 512:1024], pbigA[:, 1024:1536]]

        res = P.sbuf("res", [128, 8 * TOK], F32)
        xn = P.sbuf("xn", [128, 8 * 512], BF16)
        ident = P.sbuf("ident", [128, 128], F32)
        identb = P.sbuf("identb", [128, 128], BF16)
        wqb = P.sbuf("wqb", [128, 8 * 1024], BF16)
        kvbuf = [P.sbuf("kq0b", [33, MMAX * 128], BF16), P.sbuf("kq1b", [33, MMAX * 128], BF16)]
        vqb = P.sbuf("vqb", [128, 2 * MMAX * 33], BF16)
        bcb = P.sbuf("bcb", [128, GOFF[-1]], F32)
        dmb = P.sbuf("dmb", [128, 4 * 512], BF16)
        ksqb = P.sbuf("ksqb", [1, 512], F32)
        ksqt = P.sbuf("ksqt", [1, 2], F32)
        smax2 = P.sbuf("smax2", [1, 8], F32)
        wkuT = P.sbuf("wkuT", [128, 32], BF16)
        wvuT = P.sbuf("wvuT", [128, 32], BF16)
        wkuraw = P.sbuf("wkuraw", [32, 256], F32)
        woeff = P.sbuf("woeff", [128, 2 * 1024], BF16)
        worow = P.sbuf("worow", [128, 1024], BF16)
        qeff = P.sbuf("qeff", [128, 2 * 512], BF16)
        qsq = P.sbuf("qsq", [128, 2 * 512], BF16)
        qaug = P.sbuf("qaug", [33, 8 * 512], BF16)
        probsb = P.sbuf("probsb", [128, 2048], BF16)
        outlat = P.sbuf("outlat", [128, 2 * 512], BF16)
        recrow = P.sbuf("recrow", [1, 512], F32)
        recb = P.sbuf("recb", [128, 512], F32)
        rhsn = P.sbuf("rhsn", [128, 2 * 512], BF16)
        qh = P.sbuf("qh", [128, 512], BF16)
        indfl = P.sbuf("indfl", [1, 512], F32)
        xtst = P.sbuf("xtst", [128, 2 * DM], F32)
        A = LNArea(P)

        emit_ident(P, ident)
        P.do("vector", lambda e: e.tensor_copy(identb[:, :], ident[:, :]),
             reads=[rrf(ident)], writes=[rrf(identb)])
        P.do("vector", lambda e: e.memset(indfl[:, :], 0.0), writes=[rrf(indfl)])
        for r in range(4):
            P.do("vector", lambda e, r=r: e.memset(
                     indfl[0:1, r * 128 + r * 32: r * 128 + (r + 1) * 32], 1.0),
                 writes=[(indfl.name, r * 128 + r * 32, r * 128 + (r + 1) * 32)])
        P.do("vector", lambda e: e.memset(smax2[:, :], 0.0), writes=[rrf(smax2)])

        for k in range(8):
            P.do("gpsimd", lambda e, k=k: e.dma_start(
                     out=wqb[:, k * 1024:(k + 1) * 1024], in_=wq_ap[k * 128:(k + 1) * 128, :]),
                 reads=[("wq", k * 128 * DM, (k + 1) * 128 * DM)],
                 writes=[rr(wqb, k * 1024, (k + 1) * 1024)], dma=True, dsem="D_wqb")
        P.do("sync", lambda e: e.dma_start(out=bcb[:, :], in_=bc_ap),
             reads=[("bcols", 0, 1)], writes=[rrf(bcb)], dma=True)
        P.do("sync", lambda e: e.dma_start(out=dmb[:, :], in_=dm_ap),
             reads=[("dmask", 0, 1)], writes=[rrf(dmb)], dma=True)
        P.do("gpsimd", lambda e: e.dma_start(out=wkuraw[:, 0:128], in_=wku_ap),
             reads=[("wku", 0, 1)], writes=[rr(wkuraw, 0, 128)], dma=True, dsem="D_wku")
        P.do("gpsimd", lambda e: e.dma_start(out=wkuraw[:, 128:256], in_=wvu_ap),
             reads=[("wvu", 0, 1)], writes=[rr(wkuraw, 128, 256)], dma=True, dsem="D_wku")
        for kv in range(2):
            for i in range(16):
                P.do("sync", lambda e, kv=kv, i=i: e.dma_start(
                         out=ksqb[:, :], in_=ksq_ap[0:1, kv * 8192 + i * 512:
                                                    kv * 8192 + (i + 1) * 512]),
                     reads=[("ksqf", 0, 1)], writes=[rrf(ksqb)], dma=True, dsem="D_ksq")
                P.do("vector", lambda e: e.tensor_reduce(ksqt[0:1, 0:1], ksqb[0:1, :],
                                                         mybir.AxisListType.X, ALU.max),
                     reads=[rrf(ksqb)], writes=[rr(ksqt, 0, 1)])
                P.do("vector", lambda e, kv=kv: e.tensor_tensor(
                         smax2[0:1, kv:kv + 1], smax2[0:1, kv:kv + 1], ksqt[0:1, 0:1], ALU.max),
                     reads=[rr(smax2, kv, kv + 1), rr(ksqt, 0, 1)],
                     writes=[rr(smax2, kv, kv + 1)])
        for g in range(2, 8):
            P.do("vector", lambda e, g=g: e.tensor_copy(smax2[0:1, g:g + 1],
                                                        smax2[0:1, g // 4:g // 4 + 1]),
                 reads=[rr(smax2, g // 4, g // 4 + 1)], writes=[rr(smax2, g, g + 1)])

        SCALE = 128.0 ** -0.5
        P.do("tensor", lambda e: e.transpose(pbx[:, 0:32], wkuraw[0:32, 0:128], ident[0:32, 0:32]),
             reads=[rr(wkuraw, 0, 128), rrf(ident)], writes=[rr(pbx, 0, 32)])
        P.do("vector", lambda e: e.tensor_scalar(wkuT[:, :], pbx[:, 0:32], SCALE, None, ALU.mult),
             reads=[rr(pbx, 0, 32)], writes=[rrf(wkuT)])
        P.do("tensor", lambda e: e.transpose(pbx[:, 32:64], wkuraw[0:32, 128:256], ident[0:32, 0:32]),
             reads=[rr(wkuraw, 128, 256), rrf(ident)], writes=[rr(pbx, 32, 64)])
        P.do("vector", lambda e: e.tensor_copy(wvuT[:, :], pbx[:, 32:64]),
             reads=[rr(pbx, 32, 64)], writes=[rrf(wvuT)])
        for g in range(8):
            P.do("gpsimd", lambda e, g=g: e.dma_start(out=worow[:, :],
                                                      in_=wo_ap[g * 128:(g + 1) * 128, :]),
                 reads=[("wo", g * 128 * DM, (g + 1) * 128 * DM)],
                 writes=[rrf(worow)], dma=True, dsem="D_worow")
            for half in range(2):
                P.do("tensor", lambda e, half=half: e.matmul(
                         pbx[0:32, :], wvuT[:, :], worow[:, half * 512:(half + 1) * 512],
                         start=True, stop=True),
                     reads=[rrf(wvuT), rr(worow, half * 512, (half + 1) * 512)],
                     writes=[rr(pbx, 0, 512)])
                c = g // 4
                P.do("vector", lambda e, g=g, half=half, c=c: e.tensor_copy(
                         woeff[(g % 4) * 32:(g % 4) * 32 + 32,
                               c * 1024 + half * 512: c * 1024 + (half + 1) * 512],
                         pbx[0:32, :]),
                     reads=[rr(pbx, 0, 512)],
                     writes=[rr(woeff, c * 1024 + half * 512, c * 1024 + (half + 1) * 512)])

        emit_load_res(P, x_ap, res, ident, ldbanks, xt=xtst)

        for t in range(4):
            toff = t * 512
            M = L0A_SLOTS[t]
            ng = M // 4
            for kv in range(2):
                P.do("sync", lambda e, t=t, kv=kv, M=M: e.dma_start(
                         out=kvbuf[kv][0:33, 0:M * 128], in_=kq_aps[t][kv]),
                     reads=[(f"kq{t}", 0, 1)],
                     writes=[rr(kvbuf[kv], 0, M * 128)], dma=True, dsem=f"D_kq{kv}"),
                P.do("sync", lambda e, t=t, kv=kv, M=M: e.dma_start(
                         out=vqb[:, kv * MMAX * 33:kv * MMAX * 33 + M * 33].rearrange(
                             "p (c w) -> p c w", w=33),
                         in_=vq_aps[t][kv].rearrange("(c p) w -> p c w", p=128)),
                     reads=[(f"vq{t}", 0, 1)],
                     writes=[rr(vqb, kv * MMAX * 33, kv * MMAX * 33 + M * 33)],
                     dma=True, dsem=f"D_vq{kv}")
            emit_ln(P, A,
                    lambda c, toff=toff: res[:, c * TOK + toff: c * TOK + toff + 512],
                    lambda c, toff=toff: (res.name, c * TOK + toff, c * TOK + toff + 512),
                    xn, lnbanks)
            for h in range(8):
                for k in range(8):
                    P.do("tensor", lambda e, h=h, k=k: e.matmul(
                             pwo[:, 0:512], wqb[:, k * 1024 + h * 128: k * 1024 + (h + 1) * 128],
                             xn[:, k * 512:(k + 1) * 512], start=(k == 0), stop=(k == 7)),
                         reads=[rr(wqb, k * 1024 + h * 128, k * 1024 + (h + 1) * 128),
                                rr(xn, k * 512, (k + 1) * 512)],
                         writes=[rr(pwo, 0, 512)], defer=(k != 7))
                P.do("vector", lambda e: e.tensor_copy(qh[:, :], pwo[:, 0:512]),
                     reads=[rr(pwo, 0, 512)], writes=[rrf(qh)])
                P.do("tensor", lambda e: e.matmul(pbx[0:32, :], wkuT[:, :], qh[:, :],
                                                  start=True, stop=True),
                     reads=[rrf(wkuT), rrf(qh)], writes=[rr(pbx, 0, 512)])
                c, r = h // 4, h % 4
                P.do("vector", lambda e, c=c, r=r: e.tensor_copy(
                         qeff[r * 32:r * 32 + 32, c * 512:(c + 1) * 512], pbx[0:32, :]),
                     reads=[rr(pbx, 0, 512)], writes=[rr(qeff, c * 512, (c + 1) * 512)])
                P.do("scalar", lambda e, c=c, r=r: e.activation(
                         qsq[r * 32:r * 32 + 32, c * 512:(c + 1) * 512], pbx[0:32, :], AF.Square),
                     reads=[rr(pbx, 0, 512)], writes=[rr(qsq, c * 512, (c + 1) * 512)])
            for g in range(8):
                c, r = g // 4, g % 4
                P.do("vector", lambda e, g=g, c=c, r=r: e.tensor_copy(
                         qaug[0:32, g * 512:(g + 1) * 512],
                         qeff[r * 32:r * 32 + 32, c * 512:(c + 1) * 512]),
                     reads=[rr(qeff, c * 512, (c + 1) * 512)],
                     writes=[rr(qaug, g * 512, (g + 1) * 512)])
                P.do("tensor", lambda e, c=c, r=r: e.matmul(
                         pbx[0:1, :], A.ones_bf[r * 32:r * 32 + 32, 0:1],
                         qsq[r * 32:r * 32 + 32, c * 512:(c + 1) * 512],
                         start=True, stop=True, tile_position=(r * 32, 0)),
                     reads=[rr(qsq, c * 512, (c + 1) * 512), rrf(A.ones_bf)],
                     writes=[rr(pbx, 0, 512)])
                P.do("scalar", lambda e, g=g: e.activation(
                         qaug[32:33, g * 512:(g + 1) * 512],
                         pbx[0:1, :], AF.Sqrt, scale=smax2[0:1, g:g + 1]),
                     reads=[rr(pbx, 0, 512), rr(smax2, g, g + 1)],
                     writes=[rr(qaug, g * 512, (g + 1) * 512)])
            for c in range(2):
                for r in range(4):
                    g = c * 4 + r
                    kv = c
                    kb = kvbuf[kv]
                    for gr in range(ng):
                        diag = (gr == ng - 1)
                        for cc in range(4):
                            ch = gr * 4 + cc
                            P.do("tensor", lambda e, ch=ch, cc=cc, g=g, diag=diag,
                                 kb=kb: e.matmul(
                                     pbigA[:, cc * 512:(cc + 1) * 512],
                                     kb[0:33, ch * 128:(ch + 1) * 128],
                                     qaug[:, g * 512:(g + 1) * 512],
                                     start=True, stop=not diag),
                                 reads=[rr(kb, ch * 128, (ch + 1) * 128),
                                        rr(qaug, g * 512, (g + 1) * 512)],
                                 writes=[rr(pbigA, cc * 512, (cc + 1) * 512)], defer=True)
                            if diag:
                                P.do("tensor", lambda e, cc=cc: e.matmul(
                                         pbigA[:, cc * 512:(cc + 1) * 512], identb[:, :],
                                         dmb[:, cc * 512:(cc + 1) * 512], start=False, stop=True),
                                     reads=[rrf(identb), rr(dmb, cc * 512, (cc + 1) * 512)],
                                     writes=[rr(pbigA, cc * 512, (cc + 1) * 512)], defer=True)
                        P.do("scalar", lambda e, gr=gr, t=t: e.activation(
                                 probsb[:, :], pbigA[:, :], AF.Exp,
                                 bias=bcb[:, GOFF[t] + gr:GOFF[t] + gr + 1]),
                             reads=[rr(pbigA, 0, 2048), rr(bcb, GOFF[t] + gr, GOFF[t] + gr + 1)],
                             writes=[rrf(probsb)])
                        for cc in range(4):
                            ch = gr * 4 + cc
                            P.do("tensor", lambda e, ch=ch, cc=cc, kv=kv, M=M: e.matmul(
                                     pvb[0:33, :],
                                     vqb[:, (kv * MMAX + ch) * 33:(kv * MMAX + ch + 1) * 33],
                                     probsb[:, cc * 512:(cc + 1) * 512],
                                     start=(ch == 0), stop=(ch == M - 1)),
                                 reads=[rr(vqb, (kv * MMAX + ch) * 33, (kv * MMAX + ch + 1) * 33),
                                        rrf(probsb)],
                                 writes=[rr(pvb, 0, 512)], defer=True)
                    P.do("vector", lambda e, c=c, r=r: e.tensor_copy(
                             outlat[r * 32:r * 32 + 32, c * 512:(c + 1) * 512], pvb[0:32, :]),
                         reads=[rr(pvb, 0, 512)], writes=[rr(outlat, c * 512, (c + 1) * 512)])
                    P.do("vector", lambda e: e.reciprocal(recrow[0:1, :], pvb[32:33, :]),
                         reads=[rr(pvb, 0, 512)], writes=[rrf(recrow)])
                    P.do("tensor", lambda e, r=r: e.matmul(
                             pbx[:, :], indfl[0:1, r * 128:(r + 1) * 128], recrow[0:1, :],
                             start=(r == 0), stop=(r == 3)),
                         reads=[rrf(recrow), rrf(indfl)],
                         writes=[rr(pbx, 0, 512)], defer=(r != 3))
                P.do("vector", lambda e: e.tensor_copy(recb[:, :], pbx[:, :]),
                     reads=[rr(pbx, 0, 512)], writes=[rrf(recb)])
                P.do("vector", lambda e, c=c: e.tensor_tensor(
                         rhsn[:, c * 512:(c + 1) * 512], outlat[:, c * 512:(c + 1) * 512],
                         recb[:, :], ALU.mult),
                     reads=[rr(outlat, c * 512, (c + 1) * 512), rrf(recb)],
                     writes=[rr(rhsn, c * 512, (c + 1) * 512)])
            for mo in range(8):
                for c in range(2):
                    P.do("tensor", lambda e, c=c, mo=mo: e.matmul(
                             pwo[:, (mo % 2) * 512:(mo % 2) * 512 + 512],
                             woeff[:, c * 1024 + mo * 128: c * 1024 + mo * 128 + 128],
                             rhsn[:, c * 512:(c + 1) * 512], start=(c == 0), stop=(c == 1)),
                         reads=[rr(woeff, c * 1024 + mo * 128, c * 1024 + mo * 128 + 128),
                                rr(rhsn, c * 512, (c + 1) * 512)],
                         writes=[rr(pwo, (mo % 2) * 512, (mo % 2) * 512 + 512)],
                         defer=(c != 1))
                P.do("vector", lambda e, mo=mo, toff=toff: e.tensor_tensor(
                         res[:, mo * TOK + toff: mo * TOK + toff + 512],
                         res[:, mo * TOK + toff: mo * TOK + toff + 512],
                         pwo[:, (mo % 2) * 512:(mo % 2) * 512 + 512], ALU.add),
                     reads=[rr(pwo, (mo % 2) * 512, (mo % 2) * 512 + 512),
                            rr(res, mo * TOK + toff, mo * TOK + toff + 512)],
                     writes=[rr(res, mo * TOK + toff, mo * TOK + toff + 512)])
        emit_store_res(P, y_ap, res, ident, ldbanks, yt=xtst)
        P.emit()
    return nc



def run_l0_attn(x_full, inputs):
    if "l0akv" not in _PROG_CACHE:
        _PROG_CACHE["l0akv"] = build_l0_attn_kv()
    if "l0am" not in _PROG_CACHE:
        _PROG_CACHE["l0am"] = build_l0_attn_main()
    nckv, ncm = _PROG_CACHE["l0akv"], _PROG_CACHE["l0am"]

    indkv = np.zeros((128, 2), np.float32)
    indkv[0:32, 0] = 1.0
    indkv[32:64, 1] = 1.0
    xs = x_full.reshape(NCORES, TOK, DM)
    in_maps = [{"x": np.ascontiguousarray(xs[c]),
                "wk": inputs["L0_attn_wk_down"].astype(np.float32, copy=False),
                "wv": inputs["L0_attn_wv_down"].astype(np.float32, copy=False),
                "indkv": indkv} for c in range(NCORES)]
    r1 = run_bass_kernel_spmd(nckv, in_maps, core_ids=list(range(NCORES)))
    klat = [np.concatenate([np.asarray(r1.results[b * 4 + c]["klat"]) for c in range(4)], axis=1)
            for b in range(2)]
    ksq = [np.concatenate([np.asarray(r1.results[b * 4 + c]["ksq"]) for c in range(4)], axis=1)
           for b in range(2)]
    vT = [np.concatenate([np.asarray(r1.results[b * 4 + c]["vT"]) for c in range(4)], axis=0)
          for b in range(2)]

    dmask = np.zeros((128, 4 * 512), np.float32)
    p = np.arange(128)[:, None]
    qq = np.arange(512)[None, :]
    for cc in range(4):
        dmask[:, cc * 512:(cc + 1) * 512] = np.where(cc * 128 + p <= qq, 0.0, -1e30)
    dmask = dmask.astype(BF)

    NGRP = [M // 4 for M in L0A_SLOTS]
    ngrp_tot = sum(NGRP)
    in_maps2 = []
    for core in range(NCORES):
        b = core // 4
        chunks = l0a_chunks(core)
        xq = np.concatenate([x_full[b, g * 512:(g + 1) * 512] for g in chunks], axis=0)
        m = {"x": np.ascontiguousarray(xq.astype(np.float32, copy=False)),
             "wq": inputs["L0_attn_wq"].astype(np.float32, copy=False),
             "wo": inputs["L0_attn_wo"].astype(np.float32, copy=False),
             "wku": inputs["L0_attn_wk_up"].astype(np.float32, copy=False),
             "wvu": inputs["L0_attn_wv_up"].astype(np.float32, copy=False),
             "dmask": dmask,
             "ksqf": np.concatenate([ksq[b][0], ksq[b][1]])[None, :].astype(np.float32)}
        bcols = np.zeros((128, ngrp_tot), np.float32)
        go = 0
        for j, M in enumerate(L0A_SLOTS):
            gq = chunks[j]
            qoff = gq * 512
            nreal = gq
            ngj = M // 4
            kqs = np.zeros((2, 33, M * 128), BF)
            vqs = np.zeros((2, M * 128, 33), BF)
            for kv in range(2):
                kla = np.asarray(klat[b])[kv * 32:(kv + 1) * 32]
                if qoff > 0:
                    kqs[kv, 0:32, :qoff] = kla[:, :qoff]
                    kqs[kv, 32, :qoff] = -1.0
                    vqs[kv, :qoff, 0:32] = np.asarray(vT[b])[:qoff, kv * 32:(kv + 1) * 32]
                    vqs[kv, :qoff, 32] = 1.0
                dlo = (ngj - 1) * 512
                kqs[kv, 0:32, dlo:dlo + 512] = kla[:, qoff:qoff + 512]
                kqs[kv, 32, dlo:dlo + 512] = -1.0
                vqs[kv, dlo:dlo + 512, 0:32] = np.asarray(vT[b])[qoff:qoff + 512,
                                                                 kv * 32:(kv + 1) * 32]
                vqs[kv, dlo:dlo + 512, 32] = 1.0
            for gr in range(ngj):
                bcols[:, go + gr] = 0.0 if (gr < nreal or gr == ngj - 1) else -1e30
            m[f"kq{j}"] = kqs
            m[f"vq{j}"] = vqs
            go += ngj
        m["bcols"] = bcols
        in_maps2.append(m)
    r2 = run_bass_kernel_spmd(ncm, in_maps2, core_ids=list(range(NCORES)))
    out = np.empty_like(x_full)
    for core in range(NCORES):
        b = core // 4
        y = np.asarray(r2.results[core]["y"])
        for j, g in enumerate(l0a_chunks(core)):
            out[b, g * 512:(g + 1) * 512] = y[j * 512:(j + 1) * 512]
    return out


def kernel(**inputs):
    x = np.asarray(inputs["L0_attn_x"], dtype=np.float32)
    x = run_l0_ffn(x, inputs)
    x = run_l0_attn(x, inputs)
    x = run_l0_ffn(x, inputs)
    x = run_l1_ffn(x, inputs)
    x = run_l1_attn(x, inputs)
    x = run_l1_ffn(x, inputs)
    return x



L1A_NSLOT = 68


def build_l1_attn_kv():
    nc = bass.Bass("TRN2", target_bir_lowering=False)
    x_ap = nc.dram_tensor("x", [TOK, DM], F32, kind="ExternalInput").ap()
    wk_ap = nc.dram_tensor("wk", [DM, DM], F32, kind="ExternalInput").ap()
    wv_ap = nc.dram_tensor("wv", [DM, DM], F32, kind="ExternalInput").ap()
    ind_ap = nc.dram_tensor("ind16", [128, 128], F32, kind="ExternalInput").ap()
    kf_ap = nc.dram_tensor("kf", [DM, TOK], BF16, kind="ExternalOutput").ap()
    ksq_ap = nc.dram_tensor("ksqh", [16, TOK], F32, kind="ExternalOutput").ap()
    vT_ap = nc.dram_tensor("vT", [TOK, DM], BF16, kind="ExternalOutput").ap()

    with contextlib.ExitStack() as ctx:
        P = Prog(nc, ctx)
        banks = [P.psum(f"pb{i}", [128, 512], F32) for i in range(6)]
        pbf = P.psum("pbf", [128, 1024], BF16)
        res = P.sbuf("res", [128, 8 * TOK], F32)
        xn = P.sbuf("xn", [128, 8 * 512], BF16)
        ident = P.sbuf("ident", [128, 128], F32)
        identb = P.sbuf("identb", [128, 128], BF16)
        wkb = P.sbuf("wkb", [128, 8 * 1024], BF16)
        wvb = P.sbuf("wvb", [128, 8 * 1024], BF16)
        indb = P.sbuf("indb", [128, 128], BF16)
        kfs = P.sbuf("kfs", [128, 512], BF16)
        sqs = P.sbuf("sqs", [128, 512], BF16)
        ksqs = P.sbuf("ksqs", [16, TOK], F32)
        vts = P.sbuf("vts", [128, 128], BF16)
        A = LNArea(P)

        emit_ident(P, ident)
        P.do("vector", lambda e: e.tensor_copy(identb[:, :], ident[:, :]),
             reads=[rrf(ident)], writes=[rrf(identb)])
        for k in range(8):
            P.do("gpsimd", lambda e, k=k: e.dma_start(
                     out=wkb[:, k * 1024:(k + 1) * 1024], in_=wk_ap[k * 128:(k + 1) * 128, :]),
                 reads=[("wk", 0, 1)], writes=[rr(wkb, k * 1024, (k + 1) * 1024)],
                 dma=True, dsem="D_wk")
            P.do("gpsimd", lambda e, k=k: e.dma_start(
                     out=wvb[:, k * 1024:(k + 1) * 1024], in_=wv_ap[k * 128:(k + 1) * 128, :]),
                 reads=[("wv", 0, 1)], writes=[rr(wvb, k * 1024, (k + 1) * 1024)],
                 dma=True, dsem="D_wv")
        P.do("gpsimd", lambda e: e.dma_start(out=indb[:, :], in_=ind_ap),
             reads=[("ind16", 0, 1)], writes=[rrf(indb)], dma=True)
        emit_load_res(P, x_ap, res, ident, banks[0:4])

        for t in range(4):
            toff = t * 512
            emit_ln(P, A,
                    lambda c, toff=toff: res[:, c * TOK + toff: c * TOK + toff + 512],
                    lambda c, toff=toff: (res.name, c * TOK + toff, c * TOK + toff + 512),
                    xn, [banks[3], banks[4], banks[5]])
            for mo in range(8):
                for k in range(8):
                    P.do("tensor", lambda e, mo=mo, k=k: e.matmul(
                             banks[0][:, :], wkb[:, k * 1024 + mo * 128: k * 1024 + (mo + 1) * 128],
                             xn[:, k * 512:(k + 1) * 512], start=(k == 0), stop=(k == 7)),
                         reads=[rr(wkb, k * 1024 + mo * 128, k * 1024 + (mo + 1) * 128),
                                rr(xn, k * 512, (k + 1) * 512)],
                         writes=[rr(banks[0], 0, 512)], defer=(k != 7))
                P.do("vector", lambda e, mo=mo, toff=toff: e.tensor_copy(kfs[:, :], banks[0][:, :]),
                     reads=[rr(banks[0], 0, 512)], writes=[rrf(kfs)])
                P.do("sync", lambda e, mo=mo, toff=toff: e.dma_start(
                         out=kf_ap[mo * 128:(mo + 1) * 128, toff:toff + 512], in_=kfs[:, :]),
                     reads=[rrf(kfs)], writes=[("kf", 0, 1)], dma=True, dsem="D_kf")
                P.do("scalar", lambda e: e.activation(sqs[:, :], banks[0][:, :], AF.Square),
                     reads=[rr(banks[0], 0, 512)], writes=[rrf(sqs)])
                P.do("tensor", lambda e, mo=mo: e.matmul(
                         banks[1][0:16, :], indb[:, mo * 16:(mo + 1) * 16], sqs[:, :],
                         start=(mo == 0), stop=(mo == 7)),
                     reads=[rrf(sqs), rrf(indb)],
                     writes=[rr(banks[1], 0, 512)], defer=(mo != 7))
            P.do("vector", lambda e, toff=toff: e.tensor_copy(ksqs[:, toff:toff + 512],
                                                              banks[1][0:16, :]),
                 reads=[rr(banks[1], 0, 512)], writes=[rr(ksqs, toff, toff + 512)])
            for mo in range(8):
                for k in range(8):
                    P.do("tensor", lambda e, mo=mo, k=k: e.matmul(
                             banks[2][:, :], wvb[:, k * 1024 + mo * 128: k * 1024 + (mo + 1) * 128],
                             xn[:, k * 512:(k + 1) * 512], start=(k == 0), stop=(k == 7)),
                         reads=[rr(wvb, k * 1024 + mo * 128, k * 1024 + (mo + 1) * 128),
                                rr(xn, k * 512, (k + 1) * 512)],
                         writes=[rr(banks[2], 0, 512)], defer=(k != 7))
                P.do("vector", lambda e: e.tensor_copy(kfs[:, :], banks[2][:, :]),
                     reads=[rr(banks[2], 0, 512)], writes=[rrf(kfs)])
                for b2 in range(4):
                    P.do("tensor", lambda e, b2=b2: e.transpose(
                             pbf[:, 0:128], kfs[:, b2 * 128:(b2 + 1) * 128], identb[:, :]),
                         reads=[rrf(kfs), rrf(identb)], writes=[rr(pbf, 0, 128)])
                    P.do("vector", lambda e: e.tensor_copy(vts[:, :], pbf[:, 0:128]),
                         reads=[rr(pbf, 0, 128)], writes=[rrf(vts)])
                    P.do("sync", lambda e, mo=mo, b2=b2, toff=toff: e.dma_start(
                             out=vT_ap[toff + b2 * 128: toff + (b2 + 1) * 128,
                                       mo * 128:(mo + 1) * 128], in_=vts[:, :]),
                         reads=[rrf(vts)], writes=[("vT", 0, 1)], dma=True, dsem="D_vT")
        P.do("sync", lambda e: e.dma_start(out=ksq_ap, in_=ksqs[:, :]),
             reads=[rrf(ksqs)], writes=[("ksqh", 0, 1)], dma=True, dsem="D_ko")
        P.wait("sync", "D_ko", P.val("D_ko"))
        P.wait("sync", "D_vT", P.val("D_vT"))
        P.wait("sync", "D_kf", P.val("D_kf"))
        P.emit()
    return nc


def _hankel_ap(dram_tensor, offset):
    import bass_rust
    return bass_rust.AP(tensor=dram_tensor.ap().tensor, offset=offset,
                        ap=[[-1, 128], [1, 512]])


def build_l1_attn_main():
    nc = bass.Bass("TRN2", target_bir_lowering=False)
    NS = L1A_NSLOT
    x_ap = nc.dram_tensor("x", [TOK, DM], F32, kind="ExternalInput").ap()
    wq_ap = nc.dram_tensor("wq", [DM, DM], F32, kind="ExternalInput").ap()
    wo_ap = nc.dram_tensor("wo", [DM, DM], F32, kind="ExternalInput").ap()
    bt_ap = nc.dram_tensor("btab", [7, 16, 128, 512], BF16, kind="ExternalInput").ap()
    kst_aps = [nc.dram_tensor(f"kst{t}", [16, 67, NS * 128], BF16, kind="ExternalInput").ap()
               for t in range(4)]
    vst_aps = [nc.dram_tensor(f"vst{t}", [16, NS * 128, 65], BF16, kind="ExternalInput").ap()
               for t in range(4)]
    ksq_ap = nc.dram_tensor("ksqh", [16, 8192], F32, kind="ExternalInput").ap()
    y_ap = nc.dram_tensor("y", [TOK, DM], F32, kind="ExternalOutput").ap()

    with contextlib.ExitStack() as ctx:
        P = Prog(nc, ctx)
        pbigA = P.psum("pbigA", [128, 2048], F32)
        pvb = P.psum("pvb", [128, 512], F32)
        pbx = P.psum("pbx", [128, 512], F32)
        pwo = P.psum("pwo", [128, 1024], F32)
        ldbanks = [pbigA[:, i * 512:(i + 1) * 512] for i in range(4)]
        lnbanks = [pbigA[:, 0:512], pbigA[:, 512:1024], pbigA[:, 1024:1536]]

        res = P.sbuf("res", [128, 8 * TOK], F32)
        xn = P.sbuf("xn", [128, 8 * 512], BF16)
        ident = P.sbuf("ident", [128, 128], F32)
        identb = P.sbuf("identb", [128, 128], BF16)
        wqb = P.sbuf("wqb", [128, 8 * 1024], BF16)
        wos = P.sbuf("wos", [128, 8 * 128], BF16)
        kstb = P.sbuf("kstb", [67, NS * 128], BF16)
        vstb = P.sbuf("vstb", [128, NS * 65], BF16)
        qf = P.sbuf("qf", [128, 8 * 512], BF16)
        qsqf = P.sbuf("qsqf", [128, 8 * 512], BF16)
        qaug = P.sbuf("qaug", [67, 512], BF16)
        btile = P.sbuf("btile", [128, 7 * 512], BF16)
        probsb = P.sbuf("probsb", [128, 2048], BF16)
        outv = P.sbuf("outv", [128, 8 * 512], BF16)
        rhsnb = outv
        recrow = P.sbuf("recrow", [1, 2 * 512], F32)
        recb = P.sbuf("recb", [128, 512], F32)
        ksqb = P.sbuf("ksqb", [16, 512], F32)
        smaxh = P.sbuf("smaxh", [16, 1], F32)
        smaxt = P.sbuf("smaxt", [16, 1], F32)
        smaxrow = P.sbuf("smaxrow", [1, 16], F32)
        indq = P.sbuf("indq", [128, 2], BF16)
        ind2fl = P.sbuf("ind2fl", [1, 256], F32)
        xtst = P.sbuf("xtst", [128, 2 * DM], F32)
        A = LNArea(P)

        emit_ident(P, ident)
        P.do("vector", lambda e: e.tensor_copy(identb[:, :], ident[:, :]),
             reads=[rrf(ident)], writes=[rrf(identb)])
        P.do("vector", lambda e: e.memset(indq[:, :], 0.0), writes=[rrf(indq)])
        P.do("vector", lambda e: e.memset(indq[0:64, 0:1], 1.0), writes=[(indq.name, 0, 1)])
        P.do("vector", lambda e: e.memset(indq[64:128, 1:2], 1.0), writes=[(indq.name, 1, 2)])
        P.do("vector", lambda e: e.memset(ind2fl[:, :], 0.0), writes=[rrf(ind2fl)])
        P.do("vector", lambda e: e.memset(ind2fl[0:1, 0:64], 1.0), writes=[(ind2fl.name, 0, 64)])
        P.do("vector", lambda e: e.memset(ind2fl[0:1, 128 + 64:256], 1.0),
             writes=[(ind2fl.name, 192, 256)])
        P.do("vector", lambda e: e.memset(smaxh[:, :], 0.0), writes=[rrf(smaxh)])
        P.do("vector", lambda e: e.memset(qaug[64:67, :], 1.0), writes=[rr(qaug, 0, 512)])
        for k in range(8):
            P.do("gpsimd", lambda e, k=k: e.dma_start(
                     out=wqb[:, k * 1024:(k + 1) * 1024], in_=wq_ap[k * 128:(k + 1) * 128, :]),
                 reads=[("wq", 0, 1)], writes=[rr(wqb, k * 1024, (k + 1) * 1024)],
                 dma=True, dsem="D_wqb")
        for i in range(16):
            P.do("sync", lambda e, i=i: e.dma_start(
                     out=ksqb[:, :], in_=ksq_ap[:, i * 512:(i + 1) * 512]),
                 reads=[("ksqh", 0, 1)], writes=[rrf(ksqb)], dma=True, dsem="D_ksq")
            P.do("vector", lambda e: e.tensor_reduce(smaxt[:, 0:1], ksqb[:, :],
                                                     mybir.AxisListType.X, ALU.max),
                 reads=[rrf(ksqb)], writes=[rrf(smaxt)])
            P.do("vector", lambda e: e.tensor_tensor(smaxh[:, :], smaxh[:, :], smaxt[:, :],
                                                     ALU.max),
                 reads=[rrf(smaxt), (smaxh.name, 0, 1)], writes=[(smaxh.name, 0, 1)])
        P.do("tensor", lambda e: e.transpose(pbx[0:1, 0:16], smaxh[0:16, 0:1],
                                             ident[0:16, 0:16]),
             reads=[(smaxh.name, 0, 1), rrf(ident)], writes=[rr(pbx, 0, 16)])
        P.do("vector", lambda e: e.tensor_copy(smaxrow[:, :], pbx[0:1, 0:16]),
             reads=[rr(pbx, 0, 16)], writes=[rrf(smaxrow)])

        emit_load_res(P, x_ap, res, ident, ldbanks, xt=xtst)

        for t in range(4):
            toff = t * 512
            emit_ln(P, A,
                    lambda c, toff=toff: res[:, c * TOK + toff: c * TOK + toff + 512],
                    lambda c, toff=toff: (res.name, c * TOK + toff, c * TOK + toff + 512),
                    xn, lnbanks)
            for mo in range(8):
                for k in range(8):
                    P.do("tensor", lambda e, mo=mo, k=k: e.matmul(
                             pwo[:, 0:512], wqb[:, k * 1024 + mo * 128: k * 1024 + (mo + 1) * 128],
                             xn[:, k * 512:(k + 1) * 512], start=(k == 0), stop=(k == 7)),
                         reads=[rr(wqb, k * 1024 + mo * 128, k * 1024 + (mo + 1) * 128),
                                rr(xn, k * 512, (k + 1) * 512)],
                         writes=[rr(pwo, 0, 512)], defer=(k != 7))
                P.do("vector", lambda e, mo=mo: e.tensor_copy(qf[:, mo * 512:(mo + 1) * 512],
                                                              pwo[:, 0:512]),
                     reads=[rr(pwo, 0, 512)], writes=[rr(qf, mo * 512, (mo + 1) * 512)])
                P.do("scalar", lambda e, mo=mo: e.activation(qsqf[:, mo * 512:(mo + 1) * 512],
                                                             pwo[:, 0:512], AF.Square),
                     reads=[rr(pwo, 0, 512)], writes=[rr(qsqf, mo * 512, (mo + 1) * 512)])
            for h in range(16):
                hc, hr = h // 2, h % 2
                P.do("sync", lambda e, t=t, h=h: e.dma_start(
                         out=kstb[0:67, :], in_=kst_aps[t][h]),
                     reads=[(f"kst{t}", 0, 1)], writes=[rrf(kstb)], dma=True, dsem="D_kst")
                P.do("sync", lambda e, t=t, h=h: e.dma_start(
                         out=vstb[:, :].rearrange("p (c w) -> p c w", w=65),
                         in_=vst_aps[t][h].rearrange("(c p) w -> p c w", p=128)),
                     reads=[(f"vst{t}", 0, 1)], writes=[rrf(vstb)], dma=True, dsem="D_vst")
                P.do("vector", lambda e, hc=hc, hr=hr: e.tensor_copy(
                         qaug[0:64, :], qf[hr * 64:hr * 64 + 64, hc * 512:(hc + 1) * 512]),
                     reads=[rr(qf, hc * 512, (hc + 1) * 512)], writes=[rr(qaug, 0, 512)])
                P.do("tensor", lambda e, hc=hc, hr=hr: e.matmul(
                         pbx[0:1, :], indq[:, hr:hr + 1], qsqf[:, hc * 512:(hc + 1) * 512],
                         start=True, stop=True),
                     reads=[rr(qsqf, hc * 512, (hc + 1) * 512), rrf(indq)],
                     writes=[rr(pbx, 0, 512)])
                P.do("scalar", lambda e, h=h: e.activation(
                         qaug[64:65, :], pbx[0:1, :], AF.Sqrt, scale=smaxrow[0:1, h:h + 1]),
                     reads=[rr(pbx, 0, 512), rr(smaxrow, h, h + 1)],
                     writes=[rr(qaug, 0, 512)])
                for s in range(7):
                    P.do("sync", lambda e, s=s, h=h: e.dma_start(
                             out=btile[:, s * 512:(s + 1) * 512], in_=bt_ap[s][h]),
                         reads=[("btab", 0, 1)], writes=[rr(btile, s * 512, (s + 1) * 512)],
                         dma=True, dsem="D_bt")
                for gr in range(17):
                    for cc in range(4):
                        sl = gr * 4 + cc
                        isnear = 59 <= sl <= 65
                        P.do("tensor", lambda e, sl=sl, cc=cc: e.matmul(
                                 pbigA[:, cc * 512:(cc + 1) * 512],
                                 kstb[0:67, sl * 128:(sl + 1) * 128], qaug[:, :],
                                 start=True, stop=not isnear),
                             reads=[rr(kstb, sl * 128, (sl + 1) * 128), rr(qaug, 0, 512)],
                             writes=[rr(pbigA, cc * 512, (cc + 1) * 512)], defer=True)
                        if isnear:
                            P.do("tensor", lambda e, sl=sl, cc=cc: e.matmul(
                                     pbigA[:, cc * 512:(cc + 1) * 512], identb[:, :],
                                     btile[:, (sl - 59) * 512:(sl - 58) * 512],
                                     start=False, stop=True),
                                 reads=[rrf(identb), rr(btile, (sl - 59) * 512, (sl - 58) * 512)],
                                 writes=[rr(pbigA, cc * 512, (cc + 1) * 512)], defer=True)
                    P.do("scalar", lambda e: e.activation(probsb[:, :], pbigA[:, :], AF.Exp),
                         reads=[rr(pbigA, 0, 2048)], writes=[rrf(probsb)])
                    for cc in range(4):
                        sl = gr * 4 + cc
                        P.do("tensor", lambda e, sl=sl, cc=cc: e.matmul(
                                 pvb[0:65, :], vstb[:, sl * 65:(sl + 1) * 65],
                                 probsb[:, cc * 512:(cc + 1) * 512],
                                 start=(sl == 0), stop=(sl == NS - 1)),
                             reads=[rr(vstb, sl * 65, (sl + 1) * 65), rrf(probsb)],
                             writes=[rr(pvb, 0, 512)], defer=True)
                P.do("vector", lambda e, hc=hc, hr=hr: e.tensor_copy(
                         outv[hr * 64:hr * 64 + 64, hc * 512:(hc + 1) * 512], pvb[0:64, :]),
                     reads=[rr(pvb, 0, 512)], writes=[rr(outv, hc * 512, (hc + 1) * 512)])
                P.do("vector", lambda e, hr=hr: e.reciprocal(
                         recrow[0:1, hr * 512:(hr + 1) * 512], pvb[64:65, :]),
                     reads=[rr(pvb, 0, 512)], writes=[rr(recrow, hr * 512, (hr + 1) * 512)])
                if hr == 1:
                    for r2 in range(2):
                        P.do("tensor", lambda e, r2=r2: e.matmul(
                                 pbx[:, :], ind2fl[0:1, r2 * 128:(r2 + 1) * 128],
                                 recrow[0:1, r2 * 512:(r2 + 1) * 512],
                                 start=(r2 == 0), stop=(r2 == 1)),
                             reads=[rr(recrow, r2 * 512, (r2 + 1) * 512), rrf(ind2fl)],
                             writes=[rr(pbx, 0, 512)], defer=(r2 != 1))
                    P.do("vector", lambda e: e.tensor_copy(recb[:, :], pbx[:, :]),
                         reads=[rr(pbx, 0, 512)], writes=[rrf(recb)])
                    P.do("vector", lambda e, hc=hc: e.tensor_tensor(
                             rhsnb[:, hc * 512:(hc + 1) * 512],
                             outv[:, hc * 512:(hc + 1) * 512], recb[:, :], ALU.mult),
                         reads=[rr(outv, hc * 512, (hc + 1) * 512), rrf(recb)],
                         writes=[rr(rhsnb, hc * 512, (hc + 1) * 512)])
            for mo in range(8):
                P.do("gpsimd", lambda e, mo=mo: e.dma_start(
                         out=wos[:, :].rearrange("p (k c) -> p k c", c=128),
                         in_=wo_ap[:, mo * 128:(mo + 1) * 128].rearrange(
                             "(k p) c -> p k c", p=128)),
                     reads=[("wo", 0, 1)], writes=[rrf(wos)], dma=True, dsem="D_wos")
                for k in range(8):
                    P.do("tensor", lambda e, mo=mo, k=k: e.matmul(
                             pwo[:, (mo % 2) * 512:(mo % 2) * 512 + 512],
                             wos[:, k * 128:(k + 1) * 128],
                             rhsnb[:, k * 512:(k + 1) * 512], start=(k == 0), stop=(k == 7)),
                         reads=[rr(wos, k * 128, (k + 1) * 128),
                                rr(rhsnb, k * 512, (k + 1) * 512)],
                         writes=[rr(pwo, (mo % 2) * 512, (mo % 2) * 512 + 512)],
                         defer=(k != 7))
                P.do("vector", lambda e, mo=mo, toff=toff: e.tensor_tensor(
                         res[:, mo * TOK + toff: mo * TOK + toff + 512],
                         res[:, mo * TOK + toff: mo * TOK + toff + 512],
                         pwo[:, (mo % 2) * 512:(mo % 2) * 512 + 512], ALU.add),
                     reads=[rr(pwo, (mo % 2) * 512, (mo % 2) * 512 + 512),
                            rr(res, mo * TOK + toff, mo * TOK + toff + 512)],
                     writes=[rr(res, mo * TOK + toff, mo * TOK + toff + 512)])
        emit_store_res(P, y_ap, res, ident, ldbanks, yt=xtst)
        P.emit()
    return nc


def _t5_bucket(delta):
    import math
    nb = 16
    rb = (delta > 0).astype(np.int32) * nb
    rp = np.abs(delta)
    mx = nb // 2
    is_small = rp < mx
    rps = np.maximum(rp, 1)
    ril = mx + (np.log(rps.astype(np.float64) / mx) / math.log(128.0 / mx) * (nb - mx)).astype(np.int32)
    ril = np.minimum(ril, nb - 1)
    return rb + np.where(is_small, rp, ril)


def run_l1_attn(x_full, inputs):
    if "l1akv" not in _PROG_CACHE:
        _PROG_CACHE["l1akv"] = build_l1_attn_kv()
    if "l1am" not in _PROG_CACHE:
        _PROG_CACHE["l1am"] = build_l1_attn_main()
    nckv, ncm = _PROG_CACHE["l1akv"], _PROG_CACHE["l1am"]

    ind16 = np.zeros((128, 128), np.float32)
    for mo in range(8):
        for p in range(128):
            ind16[p, mo * 16 + 2 * mo + p // 64] = 1.0
    xs = x_full.reshape(NCORES, TOK, DM)
    in_maps = [{"x": np.ascontiguousarray(xs[c]),
                "wk": inputs["L1_attn_wk"].astype(np.float32, copy=False),
                "wv": inputs["L1_attn_wv"].astype(np.float32, copy=False),
                "ind16": ind16} for c in range(NCORES)]
    r1 = run_bass_kernel_spmd(nckv, in_maps, core_ids=list(range(NCORES)))
    kf = [np.concatenate([np.asarray(r1.results[b * 4 + c]["kf"]) for c in range(4)], axis=1)
          for b in range(2)]
    ksqh = [np.concatenate([np.asarray(r1.results[b * 4 + c]["ksqh"]) for c in range(4)], axis=1)
            for b in range(2)]
    vT = [np.concatenate([np.asarray(r1.results[b * 4 + c]["vT"]) for c in range(4)], axis=0)
          for b in range(2)]

    rel = np.asarray(inputs["L1_attn_rel_emb"]).astype(np.float32)
    p = np.arange(128)[:, None]
    qq = np.arange(512)[None, :]
    btab = np.zeros((7, 16, 128, 512), BF)
    for s in range(7):
        o = (s - 2) * 128
        delta = o + p - qq
        bk = _t5_bucket(delta)
        for h in range(16):
            btab[s, h] = rel[bk, h].astype(BF)
    c_past = rel[int(_t5_bucket(np.array([-200]))[0])]
    c_fut = rel[int(_t5_bucket(np.array([200]))[0])]

    NS = L1A_NSLOT
    in_maps2 = []
    for core in range(NCORES):
        b = core // 4
        m = {"x": np.ascontiguousarray(xs[core]),
             "wq": inputs["L1_attn_wq"].astype(np.float32, copy=False),
             "wo": inputs["L1_attn_wo"].astype(np.float32, copy=False),
             "btab": btab,
             "ksqh": np.ascontiguousarray(ksqh[b]).astype(np.float32)}
        for t in range(4):
            qoff = (core % 4) * 2048 + t * 512
            qc = qoff // 128
            nearset = [qc + d for d in range(-2, 5)]
            farset = [c for c in range(64) if c not in nearset]
            kst = np.zeros((16, 67, NS * 128), BF)
            vst = np.zeros((16, NS * 128, 65), BF)
            for h in range(16):
                krows = np.asarray(kf[b])[(h // 2) * 128 + (h % 2) * 64:
                                          (h // 2) * 128 + (h % 2) * 64 + 64]
                vrows = np.asarray(vT[b])[:, (h // 2) * 128 + (h % 2) * 64:
                                          (h // 2) * 128 + (h % 2) * 64 + 64]
                for sl in range(NS):
                    if sl < 59:
                        if sl >= len(farset):
                            kst[h, 64, sl * 128:(sl + 1) * 128] = -40.0
                            continue
                        gc = farset[sl]
                    elif sl <= 65:
                        gc = nearset[sl - 59]
                        if gc < 0 or gc >= 64:
                            kst[h, 64, sl * 128:(sl + 1) * 128] = -40.0
                            continue
                    else:
                        kst[h, 64, sl * 128:(sl + 1) * 128] = -40.0
                        continue
                    kst[h, 0:64, sl * 128:(sl + 1) * 128] = krows[:, gc * 128:(gc + 1) * 128]
                    kst[h, 64, sl * 128:(sl + 1) * 128] = -1.0
                    if sl < 59:
                        if gc < qc - 2:
                            kst[h, 65, sl * 128:(sl + 1) * 128] = np.float32(c_past[h]).astype(BF)
                        else:
                            kst[h, 66, sl * 128:(sl + 1) * 128] = np.float32(c_fut[h]).astype(BF)
                    vst[h, sl * 128:(sl + 1) * 128, 0:64] = vrows[gc * 128:(gc + 1) * 128]
                    vst[h, sl * 128:(sl + 1) * 128, 64] = 1.0
            m[f"kst{t}"] = kst
            m[f"vst{t}"] = vst
        in_maps2.append(m)
    r2 = run_bass_kernel_spmd(ncm, in_maps2, core_ids=list(range(NCORES)))
    out = np.stack([np.asarray(r2.results[c]["y"]) for c in range(NCORES)], axis=0)
    return out.reshape(2, 8192, DM)



XH = 384
REG = TOK + XH


def build_preln():
    nc = bass.Bass("TRN2", target_bir_lowering=False)
    x_ap = nc.dram_tensor("x", [TOK, DM], F32, kind="ExternalInput").ap()
    y_ap = nc.dram_tensor("y", [TOK, DM], F32, kind="ExternalOutput").ap()
    with contextlib.ExitStack() as ctx:
        P = Prog(nc, ctx)
        banks = [P.psum(f"pb{i}", [128, 512], F32) for i in range(8)]
        res = P.sbuf("res", [128, 8 * TOK], F32)
        res2 = P.sbuf("res2", [128, 8 * TOK], F32)
        ident = P.sbuf("ident", [128, 128], F32)
        xn = P.sbuf("xnf", [128, 8 * 512], F32)
        A = LNArea(P)
        emit_ident(P, ident)
        emit_load_res(P, x_ap, res, ident, banks[0:4])
        for t in range(4):
            toff = t * 512
            emit_ln(P, A,
                    lambda c, toff=toff: res[:, c * TOK + toff: c * TOK + toff + 512],
                    lambda c, toff=toff: (res.name, c * TOK + toff, c * TOK + toff + 512),
                    xn, banks[5:8])
            for c in range(8):
                P.do("vector", lambda e, c=c, toff=toff: e.tensor_copy(
                         res2[:, c * TOK + toff: c * TOK + toff + 512],
                         xn[:, c * 512:(c + 1) * 512]),
                     reads=[rr(xn, c * 512, (c + 1) * 512)],
                     writes=[rr(res2, c * TOK + toff, c * TOK + toff + 512)])
        emit_store_res(P, y_ap, res2, ident, banks[0:4])
        P.emit()
    return nc


def build_postres():
    nc = bass.Bass("TRN2", target_bir_lowering=False)
    x_ap = nc.dram_tensor("x", [TOK, DM], F32, kind="ExternalInput").ap()
    t_ap = nc.dram_tensor("t", [TOK, DM], F32, kind="ExternalInput").ap()
    y_ap = nc.dram_tensor("y", [TOK, DM], F32, kind="ExternalOutput").ap()
    with contextlib.ExitStack() as ctx:
        P = Prog(nc, ctx)
        banks = [P.psum(f"pb{i}", [128, 512], F32) for i in range(8)]
        res = P.sbuf("res", [128, 8 * TOK], F32)
        res2 = P.sbuf("res2", [128, 8 * TOK], F32)
        ident = P.sbuf("ident", [128, 128], F32)
        emit_ident(P, ident)
        emit_load_res(P, x_ap, res, ident, banks[0:4])
        emit_load_res(P, t_ap, res2, ident, banks[4:8], name="ldr2")
        for c in range(8):
            P.do("vector", lambda e, c=c: e.scalar_tensor_tensor(
                     res[:, c * TOK:(c + 1) * TOK], res2[:, c * TOK:(c + 1) * TOK], 0.5,
                     res[:, c * TOK:(c + 1) * TOK], ALU.mult, ALU.add),
                 reads=[rr(res2, c * TOK, (c + 1) * TOK), rr(res, c * TOK, (c + 1) * TOK)],
                 writes=[rr(res, c * TOK, (c + 1) * TOK)])
        emit_store_res(P, y_ap, res, ident, banks[0:4])
        P.emit()
    return nc


def build_l1sub_ffn():
    nc = bass.Bass("TRN2", target_bir_lowering=False)
    x_ap = nc.dram_tensor("x", [TOK + 128, DM], F32, kind="ExternalInput").ap()
    w1_ap = nc.dram_tensor("w1p", [DM, 44 * 128], F32, kind="ExternalInput").ap()
    w2_ap = nc.dram_tensor("w2", [2730, DM], F32, kind="ExternalInput").ap()
    vec_ap = nc.dram_tensor("vecs", [68, 128], F32, kind="ExternalInput").ap()
    y_ap = nc.dram_tensor("y", [TOK, DM], F32, kind="ExternalOutput").ap()
    NT = (TOK + 128) // 128

    with contextlib.ExitStack() as ctx:
        P = Prog(nc, ctx)
        pbigA = P.psum("pbigA", [128, 2048], F32)
        pA = P.psum("pA", [128, 512], F32)
        pB = P.psum("pB", [128, 512], F32)
        pC = P.psum("pC", [128, 1024], F32)
        ldbanks = [pbigA[:, i * 512:(i + 1) * 512] for i in range(4)]
        lnbanks = [pbigA[:, 0:512], pbigA[:, 512:1024], pbigA[:, 1024:1536]]
        res = P.sbuf("res", [128, 8 * (TOK + 128)], F32)
        xn = P.sbuf("xn", [128, 8 * 512], BF16)
        hbuf = P.sbuf("hbuf", [128, 22 * 512], BF16)
        ident = P.sbuf("ident", [128, 128], F32)
        scv = P.sbuf("scv", [128, 96], F32)
        scraw = P.sbuf("scraw", [128, 128], F32)
        ws = P.sbuf("ws", [128, 4 * 512 + 2 * 128], BF16)
        gsc = P.sbuf("gsc", [128, 512], F32)
        xtst = P.sbuf("xtst", [128, 2 * DM], F32)
        xsv = P.sbuf("xsv", [128, 4], F32)
        xstg = P.sbuf("xstg", [128, 4 * 512], F32)
        A = LNArea(P)
        RT = TOK + 128

        emit_ident(P, ident)
        P.do("sync", lambda e: e.dma_start(out=scraw[0:68, :], in_=vec_ap),
             reads=[("vecs", 0, 1)], writes=[rr(scraw, 0, 128)], dma=True)
        P.do("tensor", lambda e: e.transpose(pA[:, 0:68], scraw[0:68, :], ident[0:68, 0:68]),
             reads=[rr(scraw, 0, 128), rrf(ident)], writes=[rr(pA, 0, 68)])
        P.do("vector", lambda e: e.tensor_copy(scv[:, 0:68], pA[:, 0:68]),
             reads=[rr(pA, 0, 68)], writes=[rr(scv, 0, 96)])
        emit_load_res(P, x_ap, res, ident, ldbanks, n_tok=RT, res_ntok=RT, xt=xtst)

        W1 = w1_ap
        for t in range(4):
            toff = 128 + t * 512
            if t >= 1:
                for c in range(4, 8):
                    off = c * RT + toff - 1
                    P.do("vector", lambda e, c=c: e.tensor_copy(
                             xstg[:, (c - 4) * 512:(c - 4) * 512 + 1],
                             xsv[:, c - 4:c - 3]),
                         reads=[rr(xsv, c - 4, c - 3)],
                         writes=[rr(xstg, (c - 4) * 512, (c - 4) * 512 + 1)])
                    P.do("vector", lambda e, c=c, off=off: e.tensor_copy(
                             xstg[:, (c - 4) * 512 + 1:(c - 3) * 512],
                             res[:, off + 1: off + 512]),
                         reads=[(res.name, off + 1, off + 512)],
                         writes=[rr(xstg, (c - 4) * 512 + 1, (c - 3) * 512)])
            for c in range(4, 8):
                col = c * RT + toff + 511
                P.do("vector", lambda e, c=c, col=col: e.tensor_copy(
                         xsv[:, c - 4:c - 3], res[:, col:col + 1]),
                     reads=[(res.name, col, col + 1)],
                     writes=[rr(xsv, c - 4, c - 3)])
            def srcf(c, toff=toff, t=t):
                if c >= 4 and t >= 1:
                    return xstg[:, (c - 4) * 512:(c - 3) * 512]
                off = c * RT + toff - (1 if c >= 4 else 0)
                return res[:, off: off + 512]
            def srcr(c, toff=toff, t=t):
                if c >= 4 and t >= 1:
                    return rr(xstg, (c - 4) * 512, (c - 3) * 512)
                off = c * RT + toff - (1 if c >= 4 else 0)
                return (res.name, off, off + 512)
            emit_ln(P, A, srcf, srcr, xn, lnbanks,
                    gamma_col=lambda c: (scv[:, c:c + 1], rr(scv, 0, 96)),
                    beta_col=lambda c: (scv[:, 8 + c:9 + c], rr(scv, 0, 96)))
            for j in range(22):
                jw = 42 if j == 21 else 128
                for k in range(8):
                    slot = k % 2
                    P.do("gpsimd", lambda e, j=j, k=k, slot=slot: e.dma_start(
                             out=ws[:, slot * 1024:slot * 1024 + 256],
                             in_=W1[k * 128:(k + 1) * 128, j * 256:(j + 1) * 256]),
                         reads=[("w1p", 0, 1)],
                         writes=[rr(ws, slot * 1024, slot * 1024 + 256)], dma=True,
                         dsem=f"D_w1_{slot}")
                    P.do("tensor", lambda e, k=k, slot=slot, jw=jw: e.matmul(
                             pA[0:jw, :], ws[:, slot * 1024: slot * 1024 + jw],
                             xn[:, k * 512:(k + 1) * 512], start=(k == 0), stop=(k == 7)),
                         reads=[rr(ws, slot * 1024, slot * 1024 + 128),
                                rr(xn, k * 512, (k + 1) * 512)],
                         writes=[rr(pA, 0, 512)], defer=(k != 7))
                    P.do("tensor", lambda e, k=k, slot=slot, jw=jw: e.matmul(
                             pB[0:jw, :], ws[:, slot * 1024 + 128: slot * 1024 + 128 + jw],
                             xn[:, k * 512:(k + 1) * 512], start=(k == 0), stop=(k == 7)),
                         reads=[rr(ws, slot * 1024 + 128, slot * 1024 + 256),
                                rr(xn, k * 512, (k + 1) * 512)],
                         writes=[rr(pB, 0, 512)], defer=(k != 7))
                P.do("scalar", lambda e, j=j, jw=jw: e.activation(
                         gsc[0:jw, :], pB[0:jw, :], AF.Gelu, bias=scv[0:jw, 38 + j:39 + j]),
                     reads=[rr(pB, 0, 512), rr(scv, 0, 96)], writes=[rrf(gsc)])
                P.do("vector", lambda e, j=j, jw=jw: e.scalar_tensor_tensor(
                         hbuf[0:jw, j * 512:(j + 1) * 512], pA[0:jw, :],
                         scv[0:jw, 16 + j:17 + j], gsc[0:jw, :], ALU.add, ALU.mult),
                     reads=[rr(pA, 0, 512), rr(scv, 0, 96), rrf(gsc)],
                     writes=[rr(hbuf, j * 512, (j + 1) * 512)])
            for mo in range(8):
                for k in range(22):
                    kw = 42 if k == 21 else 128
                    slot = k % 2
                    P.do("gpsimd", lambda e, mo=mo, k=k, slot=slot, kw=kw: e.dma_start(
                             out=ws[0:kw, 2048 + slot * 128:2048 + slot * 128 + 128],
                             in_=w2_ap[k * 128:k * 128 + kw, mo * 128:(mo + 1) * 128]),
                         reads=[("w2", 0, 1)],
                         writes=[rr(ws, 2048 + slot * 128, 2048 + slot * 128 + 128)],
                         dma=True, dsem=f"D_w2_{slot}")
                    P.do("tensor", lambda e, k=k, slot=slot, kw=kw: e.matmul(
                             pC[:, 0:512], ws[0:kw, 2048 + slot * 128:2048 + slot * 128 + 128],
                             hbuf[0:kw, k * 512:(k + 1) * 512], start=(k == 0), stop=(k == 21)),
                         reads=[rr(ws, 2048 + slot * 128, 2048 + slot * 128 + 128),
                                rr(hbuf, k * 512, (k + 1) * 512)],
                         writes=[rr(pC, 0, 512)], defer=(k != 21))
                P.do("vector", lambda e, mo=mo, toff=toff: e.scalar_tensor_tensor(
                         res[:, mo * RT + toff: mo * RT + toff + 512], pC[:, 0:512],
                         scv[:, 60 + mo:61 + mo],
                         res[:, mo * RT + toff: mo * RT + toff + 512], ALU.add, ALU.add),
                     reads=[rr(pC, 0, 512), rr(scv, 0, 96),
                            rr(res, mo * RT + toff, mo * RT + toff + 512)],
                     writes=[rr(res, mo * RT + toff, mo * RT + toff + 512)])
        yv = res[:, :].rearrange("p (c t) -> p c t", t=RT)[:, :, 128:RT]
        emit_store_res_region(P, y_ap, res, ident, ldbanks, RT, 128, xtst)
        P.emit()
    return nc


def emit_store_res_region(P, y_ap, res, ident, banks, rt, col0, yt):
    nt = TOK // 128
    ynm = y_ap.tensor.name
    for i in range(nt):
        for c in range(8):
            bank = banks[(i % 2) * 2 + (c // 4)]
            P.do("tensor", lambda e, b=bank, c=c, i=i: e.transpose(
                     b[:, (c % 4) * 128:(c % 4) * 128 + 128],
                     res[:, c * rt + col0 + i * 128: c * rt + col0 + (i + 1) * 128], ident[:, :]),
                 reads=[rr(res, c * rt + col0 + i * 128, c * rt + col0 + (i + 1) * 128), rrf(ident)],
                 writes=[rr(bank, (c % 4) * 128, (c % 4) * 128 + 128)])
        buf = i % 2
        for half in range(2):
            bank = banks[(i % 2) * 2 + half]
            P.do("vector", lambda e, d=yt[:, buf * DM + half * 512: buf * DM + (half + 1) * 512],
                 s=bank[:, :]: e.tensor_copy(d, s),
                 reads=[rr(bank, 0, 512)],
                 writes=[rr(yt, buf * DM + half * 512, buf * DM + (half + 1) * 512)])
        P.do("sync", lambda e, buf=buf, i=i: e.dma_start(
                 out=y_ap[i * 128:(i + 1) * 128, :], in_=yt[:, buf * DM:(buf + 1) * DM]),
             reads=[rr(yt, buf * DM, (buf + 1) * DM)],
             writes=[(ynm, i * 128 * DM, (i + 1) * 128 * DM)], dma=True, dsem="D_st_" + ynm)
    P.wait("sync", "D_st_" + ynm, P.val("D_st_" + ynm))


REG2 = TOK + 512


def build_l1sub_attn():
    nc = bass.Bass("TRN2", target_bir_lowering=False)
    x_ap = nc.dram_tensor("x", [REG2, DM], F32, kind="ExternalInput").ap()
    wqkv_ap = nc.dram_tensor("wqkv", [DM, 3072], F32, kind="ExternalInput").ap()
    wo_ap = nc.dram_tensor("wo", [1024, DM], F32, kind="ExternalInput").ap()
    aux_ap = nc.dram_tensor("aux", [128, 4], F32, kind="ExternalInput").ap()
    tab_ap = nc.dram_tensor("tabs", [4, 128, REG2], F32, kind="ExternalInput").ap()
    mt_ap = nc.dram_tensor("mtiles", [128, 48 * 128], BF16, kind="ExternalInput").ap()
    pm_ap = nc.dram_tensor("pmrot", [128, 128], F32, kind="ExternalInput").ap()
    ind16_ap = nc.dram_tensor("ind16", [128, 128], F32, kind="ExternalInput").ap()
    indT_ap = nc.dram_tensor("indT", [16, 8 * 128], F32, kind="ExternalInput").ap()
    y_ap = nc.dram_tensor("y", [TOK, DM], F32, kind="ExternalOutput").ap()

    with contextlib.ExitStack() as ctx:
        P = Prog(nc, ctx)
        pbigA = P.psum("pbigA", [128, 2048], F32)
        pA = P.psum("pA", [128, 512], F32)
        pB = P.psum("pB", [128, 512], F32)
        pC = P.psum("pC", [128, 512], F32)
        pT = P.psum("pT", [128, 512], BF16)
        ldbanks = [pbigA[:, i * 512:(i + 1) * 512] for i in range(4)]
        res = P.sbuf("res", [128, 8 * REG2], F32)
        xn = P.sbuf("xn", [128, 8 * 512], BF16)
        ident = P.sbuf("ident", [128, 128], F32)
        identb = P.sbuf("identb", [128, 128], BF16)
        pmb = P.sbuf("pmb", [128, 128], BF16)
        ind16b = P.sbuf("ind16b", [128, 128], BF16)
        indTb = P.sbuf("indTb", [16, 8 * 128], F32)
        auxb = P.sbuf("auxb", [128, 4], F32)
        tabst = P.sbuf("tabst", [128, 512], F32)
        tabb = P.sbuf("tabb", [128, 4 * 512], F32)
        mtb = P.sbuf("mtb", [128, 48 * 128], BF16)
        qraw = P.sbuf("qraw", [128, 8 * 512], BF16)
        kraw = P.sbuf("kraw", [128, 8 * 512], BF16)
        sqs = P.sbuf("sqs", [128, 512], BF16)
        qp = P.sbuf("qp", [128, 8 * 512], BF16)
        kwin = P.sbuf("kwin", [128, 8 * 768], BF16)
        vTwin = P.sbuf("vTwin", [128, 6 * 1024], BF16)
        vstage = P.sbuf("vstage", [128, 512], BF16)
        attout = P.sbuf("attout", [128, 8 * 512], BF16)
        rn = P.sbuf("rn", [16, 512], F32)
        rowt = P.sbuf("rowt", [128, 4], F32)
        probs = P.sbuf("probs", [128, 384], BF16)
        probsf = P.sbuf("probsf", [128, 384], F32)
        probsT = P.sbuf("probsT", [128, 384], BF16)
        ws = P.sbuf("ws", [128, 4 * 512 + 256], BF16)
        sc1 = P.sbuf("sc1", [128, 512], F32)
        xtst = P.sbuf("xtst", [128, 2 * DM], F32)
        xsva = P.sbuf("xsva", [128, 4], F32)

        emit_ident(P, ident)
        P.do("vector", lambda e: e.tensor_copy(identb[:, :], ident[:, :]),
             reads=[rrf(ident)], writes=[rrf(identb)])
        for nm, apx, dst in [("pmrot", pm_ap, pmb), ("ind16", ind16_ap, ind16b)]:
            P.do("gpsimd", lambda e, apx=apx, dst=dst: e.dma_start(out=dst[:, :], in_=apx),
                 reads=[(nm, 0, 1)], writes=[rrf(dst)], dma=True, dsem="D_" + nm)
        P.do("sync", lambda e: e.dma_start(out=indTb[:, :], in_=indT_ap),
             reads=[("indT", 0, 1)], writes=[rrf(indTb)], dma=True)
        P.do("sync", lambda e: e.dma_start(out=auxb[:, :], in_=aux_ap),
             reads=[("aux", 0, 1)], writes=[rrf(auxb)], dma=True)
        P.do("sync", lambda e: e.dma_start(out=mtb[:, :], in_=mt_ap),
             reads=[("mtiles", 0, 1)], writes=[rrf(mtb)], dma=True)
        emit_load_res(P, x_ap, res, ident, ldbanks, n_tok=REG2, res_ntok=REG2, xt=xtst)

        for j in range(5):
            col0 = j * 512
            for ti in range(4):
                P.do("sync", lambda e, ti=ti, col0=col0: e.dma_start(
                         out=tabst[:, :], in_=tab_ap[ti][:, col0:col0 + 512]),
                     reads=[("tabs", 0, 1)], writes=[rrf(tabst)], dma=True, dsem="D_tab")
                ac = [0, 1, 2, 3][ti] if False else [0, 1, 2, 3][ti]
                acol = {0: 0, 1: 1, 2: 2, 3: 3}[ti]
                P.do("vector", lambda e, ti=ti, acol=acol: e.tensor_scalar(
                         tabb[:, ti * 512:(ti + 1) * 512], tabst[:, :],
                         auxb[:, acol:acol + 1], None, ALU.mult),
                     reads=[rrf(tabst), rrf(auxb)],
                     writes=[rr(tabb, ti * 512, (ti + 1) * 512)])
            for c in range(8):
                off = c * REG2 + col0 - (1 if c >= 4 else 0)
                if c >= 4 and col0 == 0:
                    P.do("vector", lambda e, c=c: e.tensor_copy(
                             xn[:, c * 512:c * 512 + 1], res[:, c * REG2:c * REG2 + 1]),
                         reads=[(res.name, c * REG2, c * REG2 + 1)],
                         writes=[rr(xn, c * 512, c * 512 + 1)])
                    P.do("vector", lambda e, c=c: e.tensor_copy(
                             xn[:, c * 512 + 1:(c + 1) * 512],
                             res[:, c * REG2:c * REG2 + 511]),
                         reads=[(res.name, c * REG2, c * REG2 + 511)],
                         writes=[rr(xn, c * 512 + 1, (c + 1) * 512)])
                    continue
                if c >= 4 and j >= 2:
                    P.do("vector", lambda e, c=c: e.tensor_copy(
                             xn[:, c * 512:c * 512 + 1], xsva[:, c - 4:c - 3]),
                         reads=[rr(xsva, c - 4, c - 3)],
                         writes=[rr(xn, c * 512, c * 512 + 1)])
                    P.do("vector", lambda e, c=c, off=off: e.tensor_copy(
                             xn[:, c * 512 + 1:(c + 1) * 512], res[:, off + 1: off + 512]),
                         reads=[(res.name, off + 1, off + 512)],
                         writes=[rr(xn, c * 512 + 1, (c + 1) * 512)])
                    continue
                P.do("vector", lambda e, c=c, off=off: e.tensor_copy(
                         xn[:, c * 512:(c + 1) * 512], res[:, off: off + 512]),
                     reads=[(res.name, off, off + 512)],
                     writes=[rr(xn, c * 512, (c + 1) * 512)])
            if 1 <= j <= 3:
                for c in range(4, 8):
                    col = c * REG2 + col0 + 511
                    P.do("vector", lambda e, c=c, col=col: e.tensor_copy(
                             xsva[:, c - 4:c - 3], res[:, col:col + 1]),
                         reads=[(res.name, col, col + 1)],
                         writes=[rr(xsva, c - 4, c - 3)])
            for mo in range(24):
                if j == 0 and mo < 8:
                    continue
                for k in range(8):
                    slot = k % 2
                    P.do("gpsimd", lambda e, mo=mo, k=k, slot=slot: e.dma_start(
                             out=ws[:, slot * 1024:slot * 1024 + 128],
                             in_=wqkv_ap[k * 128:(k + 1) * 128, mo * 128:(mo + 1) * 128]),
                         reads=[("wqkv", 0, 1)],
                         writes=[rr(ws, slot * 1024, slot * 1024 + 128)], dma=True,
                         dsem=f"D_wq{slot}")
                    P.do("tensor", lambda e, k=k, slot=slot: e.matmul(
                             pA[:, :], ws[:, slot * 1024:slot * 1024 + 128],
                             xn[:, k * 512:(k + 1) * 512], start=(k == 0), stop=(k == 7)),
                         reads=[rr(ws, slot * 1024, slot * 1024 + 128),
                                rr(xn, k * 512, (k + 1) * 512)],
                         writes=[rr(pA, 0, 512)], defer=(k != 7))
                if mo < 8:
                    P.do("vector", lambda e, mo=mo: e.tensor_copy(
                             qraw[:, mo * 512:(mo + 1) * 512], pA[:, :]),
                         reads=[rr(pA, 0, 512)], writes=[rr(qraw, mo * 512, (mo + 1) * 512)])
                elif mo < 16:
                    P.do("vector", lambda e, mo=mo: e.tensor_copy(
                             kraw[:, (mo - 8) * 512:(mo - 7) * 512], pA[:, :]),
                         reads=[rr(pA, 0, 512)],
                         writes=[rr(kraw, (mo - 8) * 512, (mo - 7) * 512)])
                else:
                    P.do("vector", lambda e: e.tensor_copy(vstage[:, :], pA[:, :]),
                         reads=[rr(pA, 0, 512)], writes=[rrf(vstage)])
                    c = mo - 16
                    for b2 in range(4):
                        P.do("tensor", lambda e, b2=b2: e.transpose(
                                 pT[:, 0:128], vstage[:, b2 * 128:(b2 + 1) * 128], identb[:, :]),
                             reads=[rrf(vstage), rrf(identb)], writes=[rr(pT, 0, 128)])
                        P.do("vector", lambda e, b2=b2, c=c: e.tensor_copy(
                                 vTwin[:, (2 + b2) * 1024 + c * 128:(2 + b2) * 1024 + (c + 1) * 128],
                                 pT[:, 0:128]),
                             reads=[rr(pT, 0, 128)],
                             writes=[rr(vTwin, (2 + b2) * 1024 + c * 128,
                                        (2 + b2) * 1024 + (c + 1) * 128)])
            for which, raw in (("q", qraw), ("k", kraw)):
                if j == 0 and which == "q":
                    continue
                for c in range(8):
                    P.do("scalar", lambda e, c=c, raw=raw: e.activation(
                             sqs[:, :], raw[:, c * 512:(c + 1) * 512], AF.Square),
                         reads=[rr(raw, c * 512, (c + 1) * 512)], writes=[rrf(sqs)])
                    P.do("tensor", lambda e, c=c: e.matmul(
                             pB[0:16, :], ind16b[:, c * 16:(c + 1) * 16], sqs[:, :],
                             start=(c == 0), stop=(c == 7)),
                         reads=[rrf(sqs), rrf(ind16b)], writes=[rr(pB, 0, 512)],
                         defer=(c != 7))
                P.do("scalar", lambda e: e.activation(rn[:, :], pB[0:16, :], AF.Sqrt),
                     reads=[rr(pB, 0, 512)], writes=[rrf(rn)])
                P.do("vector", lambda e: e.tensor_scalar(rn[:, :], rn[:, :], 1e-12, None,
                                                         ALU.max),
                     reads=[rrf(rn)], writes=[rrf(rn)])
                P.do("vector", lambda e: e.reciprocal(rn[:, :], rn[:, :]),
                     reads=[rrf(rn)], writes=[rrf(rn)])
                tb = 0 if which == "q" else 2
                dst = qp if which == "q" else kwin
                for c in range(8):
                    P.do("tensor", lambda e, c=c, raw=raw: e.matmul(
                             pC[:, :], pmb[:, :],
                             raw[:, c * 512:(c + 1) * 512].bitcast(BF16) if False
                             else raw[:, c * 512:(c + 1) * 512],
                             start=True, stop=True),
                         reads=[rr(raw, c * 512, (c + 1) * 512), rrf(pmb)],
                         writes=[rr(pC, 0, 512)])
                    P.do("tensor", lambda e, c=c: e.matmul(
                             pB[:, :], indTb[:, c * 128:(c + 1) * 128], rn[:, :],
                             start=True, stop=True),
                         reads=[rrf(rn), rrf(indTb)], writes=[rr(pB, 0, 512)])
                    P.do("vector", lambda e, c=c, raw=raw, tb=tb: e.tensor_tensor(
                             sc1[:, :], raw[:, c * 512:(c + 1) * 512],
                             tabb[:, tb * 512:(tb + 1) * 512], ALU.mult),
                         reads=[rr(raw, c * 512, (c + 1) * 512), rr(tabb, tb * 512, (tb + 1) * 512)],
                         writes=[rrf(sc1)])
                    P.do("vector", lambda e, c=c, tb=tb: e.tensor_tensor(
                             tabst[:, :], pC[:, :], tabb[:, (tb + 1) * 512:(tb + 2) * 512],
                             ALU.mult),
                         reads=[rr(pC, 0, 512), rr(tabb, (tb + 1) * 512, (tb + 2) * 512)],
                         writes=[rrf(tabst)])
                    P.do("vector", lambda e: e.tensor_tensor(sc1[:, :], sc1[:, :],
                                                             tabst[:, :], ALU.add),
                         reads=[rrf(sc1), rrf(tabst)], writes=[rrf(sc1)])
                    if which == "q":
                        P.do("vector", lambda e, c=c: e.tensor_tensor(
                                 qp[:, c * 512:(c + 1) * 512], sc1[:, :], pB[:, :], ALU.mult),
                             reads=[rrf(sc1), rr(pB, 0, 512)],
                             writes=[rr(qp, c * 512, (c + 1) * 512)])
                    else:
                        P.do("vector", lambda e, c=c: e.tensor_tensor(
                                 kwin[:, c * 768 + 256: c * 768 + 768], sc1[:, :],
                                 pB[:, :], ALU.mult),
                             reads=[rrf(sc1), rr(pB, 0, 512)],
                             writes=[rr(kwin, c * 768 + 256, c * 768 + 768)])
            if j >= 1:
                for h in range(16):
                    hc, hr = h // 2, h % 2
                    for qb in range(4):
                        gqb = (j - 1) * 4 + qb
                        P.do("tensor", lambda e, hc=hc, hr=hr, qb=qb: e.matmul(
                                 pA[:, 0:384],
                                 qp[hr * 64:hr * 64 + 64, hc * 512 + qb * 128: hc * 512 + (qb + 1) * 128],
                                 kwin[hr * 64:hr * 64 + 64,
                                      hc * 768 + qb * 128: hc * 768 + qb * 128 + 384],
                                 start=True, stop=True),
                             reads=[rr(qp, hc * 512, (hc + 1) * 512), rr(kwin, hc * 768, (hc + 1) * 768)],
                             writes=[rr(pA, 0, 384)], defer=True)
                        P.do("vector", lambda e, gqb=gqb: e.tensor_tensor(
                                 pA[:, 0:384], pA[:, 0:384],
                                 mtb[:, gqb * 384:(gqb + 1) * 384], ALU.add),
                             reads=[rr(pA, 0, 384), rrf(mtb)],
                             writes=[rr(pA, 0, 384)])
                        P.do("vector", lambda e: e.tensor_reduce(
                                 rowt[:, 0:1], pA[:, 0:384], mybir.AxisListType.X, ALU.max,
                                 negate=True),
                             reads=[rr(pA, 0, 384)], writes=[rr(rowt, 0, 1)])
                        P.do("scalar", lambda e: e.activation(
                                 probsf[:, :], pA[:, 0:384], AF.Exp, bias=rowt[:, 0:1],
                                 accum_out=rowt[:, 1:2]),
                             reads=[rr(pA, 0, 384), rr(rowt, 0, 1)],
                             writes=[rrf(probsf), (rowt.name, 1, 2)])
                        P.do("vector", lambda e: e.reciprocal(rowt[:, 2:3], rowt[:, 1:2]),
                             reads=[(rowt.name, 1, 2)], writes=[(rowt.name, 2, 3)])
                        P.do("vector", lambda e: e.tensor_scalar(
                                 probs[:, :], probsf[:, :], rowt[:, 2:3], None, ALU.mult),
                             reads=[rrf(probsf), (rowt.name, 2, 3)], writes=[rrf(probs)])
                        for ch in range(3):
                            P.do("tensor", lambda e, ch=ch: e.transpose(
                                     pT[:, 0:128], probs[:, ch * 128:(ch + 1) * 128],
                                     identb[:, :]),
                                 reads=[rrf(probs), rrf(identb)], writes=[rr(pT, 0, 128)])
                            P.do("vector", lambda e, ch=ch: e.tensor_copy(
                                     probsT[:, ch * 128:(ch + 1) * 128], pT[:, 0:128]),
                                 reads=[rr(pT, 0, 128)],
                                 writes=[rr(probsT, ch * 128, (ch + 1) * 128)])
                        for ch in range(3):
                            P.do("tensor", lambda e, ch=ch, qb=qb, h=h: e.matmul(
                                     pB[0:64, :].rearrange("p n -> p n") if False
                                     else pB[0:64, 0:128],
                                     vTwin[:, (qb + ch) * 1024 + h * 64:(qb + ch) * 1024 + (h + 1) * 64],
                                     probsT[:, ch * 128:(ch + 1) * 128],
                                     start=(ch == 0), stop=(ch == 2)),
                                 reads=[rr(vTwin, (qb + ch) * 1024, (qb + ch + 1) * 1024),
                                        rr(probsT, ch * 128, (ch + 1) * 128)],
                                 writes=[rr(pB, 0, 128)], defer=(ch != 2))
                        P.do("vector", lambda e, hr=hr, hc=hc, qb=qb: e.tensor_copy(
                                 attout[hr * 64:hr * 64 + 64,
                                        hc * 512 + qb * 128: hc * 512 + (qb + 1) * 128],
                                 pB[0:64, 0:128]),
                             reads=[rr(pB, 0, 128)],
                             writes=[rr(attout, hc * 512 + qb * 128, hc * 512 + (qb + 1) * 128)])
                for mo in range(8):
                    for k in range(8):
                        slot = k % 2
                        P.do("gpsimd", lambda e, mo=mo, k=k, slot=slot: e.dma_start(
                                 out=ws[:, 2048 + slot * 128:2048 + slot * 128 + 128],
                                 in_=wo_ap[k * 128:(k + 1) * 128, mo * 128:(mo + 1) * 128]),
                             reads=[("wo", 0, 1)],
                             writes=[rr(ws, 2048 + slot * 128, 2048 + slot * 128 + 128)],
                             dma=True, dsem=f"D_wo{slot}")
                        P.do("tensor", lambda e, k=k, slot=slot: e.matmul(
                                 pC[:, :], ws[:, 2048 + slot * 128:2048 + slot * 128 + 128],
                                 attout[:, k * 512:(k + 1) * 512], start=(k == 0), stop=(k == 7)),
                             reads=[rr(ws, 2048 + slot * 128, 2048 + slot * 128 + 128),
                                    rr(attout, k * 512, (k + 1) * 512)],
                             writes=[rr(pC, 0, 512)], defer=(k != 7))
                    off = mo * REG2 + col0
                    P.do("vector", lambda e, off=off: e.tensor_tensor(
                             res[:, off:off + 512], res[:, off:off + 512], pC[:, :], ALU.add),
                         reads=[rr(pC, 0, 512), (res.name, off, off + 512)],
                         writes=[(res.name, off, off + 512)])
            for c in range(8):
                P.do("vector", lambda e, c=c: e.tensor_copy(
                         kwin[:, c * 768:c * 768 + 256], kwin[:, c * 768 + 512:c * 768 + 768]),
                     reads=[rr(kwin, c * 768 + 512, c * 768 + 768)],
                     writes=[rr(kwin, c * 768, c * 768 + 256)])
            for b2 in range(2):
                P.do("vector", lambda e, b2=b2: e.tensor_copy(
                         vTwin[:, b2 * 1024:(b2 + 1) * 1024],
                         vTwin[:, (4 + b2) * 1024:(5 + b2) * 1024]),
                     reads=[rr(vTwin, (4 + b2) * 1024, (5 + b2) * 1024)],
                     writes=[rr(vTwin, b2 * 1024, (b2 + 1) * 1024)])
        emit_store_res_region(P, y_ap, res, ident, ldbanks, REG2, 512, xtst)
        P.emit()
    return nc



def _l1_tables(core):
    n = 8192
    start = (core % 4) * 2048
    pos = np.arange(start - 512, start + 2048, dtype=np.float64)
    half = np.arange(0, 64, 2, dtype=np.float64)
    inv_freq = 1.0 / 10000.0 ** (half / 64.0)
    freqs = pos[:, None] * inv_freq[None, :]
    freqs = np.concatenate([freqs, freqs], axis=1)
    base_scale = (half + 0.4 * 64) / (1.4 * 64)
    power = (pos - n // 2) / (256 // 2)
    scale = base_scale[None, :] ** power[:, None]
    scale = np.concatenate([scale, scale], axis=1)
    cos, sin = np.cos(freqs), np.sin(freqs)
    qc = (cos * scale * 8.0).T
    qs = (sin * scale * 8.0).T
    kc = (cos / scale).T
    ks = (sin / scale).T
    out = np.zeros((4, 128, 2560), np.float32)
    for i, tbl in enumerate((qc, qs, kc, ks)):
        out[i, 0:64] = tbl
        out[i, 64:128] = tbl
    return out


def _l1_masks(core):
    start = (core % 4) * 2048
    mt = np.zeros((128, 48 * 128), np.float32)
    qp = np.arange(128)[:, None]
    kj = np.arange(128)[None, :]
    for gqb in range(16):
        for ch in range(3):
            d = 256 - ch * 128 + qp - kj
            bad = (d < 0) | (d > 256)
            kglob = start + gqb * 128 - 256 + ch * 128 + kj
            bad = bad | (kglob < 0)
            mt[:, (gqb * 3 + ch) * 128:(gqb * 3 + ch + 1) * 128] = np.where(bad, -1e30, 0.0)
    return mt.astype(BF)


def _ind16_mats():
    ind16 = np.zeros((128, 128), np.float32)
    indT = np.zeros((16, 8 * 128), np.float32)
    for c in range(8):
        for p in range(128):
            h = 2 * c + p // 64
            ind16[p, c * 16 + h] = 1.0
            indT[h, c * 128 + p] = 1.0
    return ind16, indT


def _pm_rot():
    pm = np.zeros((128, 128), np.float32)
    for blk in range(2):
        o = blk * 64
        for d in range(32):
            pm[o + d + 32, o + d] = -1.0
            pm[o + d, o + d + 32] = 1.0
    return pm


def run_l1_ffn(x_full, inputs):
    for nm, fn in (("preln", build_preln), ("postres", build_postres),
                   ("suba", build_l1sub_attn), ("subf", build_l1sub_ffn)):
        if nm not in _PROG_CACHE:
            _PROG_CACHE[nm] = fn()
    xs = np.ascontiguousarray(x_full.reshape(NCORES, TOK, DM))
    r = run_bass_kernel_spmd(_PROG_CACHE["preln"], [{"x": xs[c]} for c in range(NCORES)],
                             core_ids=list(range(NCORES)))
    t = np.stack([np.asarray(r.results[c]["y"]) for c in range(NCORES)])
    ind16, indT = _ind16_mats()
    pm = _pm_rot()
    tabs = [_l1_tables(c) for c in range(NCORES)]
    masks = [_l1_masks(c) for c in range(NCORES)]
    for i in range(6):
        qs = np.asarray(inputs["L1_ffn_q_scale"][i], dtype=np.float32)
        ks = np.asarray(inputs["L1_ffn_k_scale"][i], dtype=np.float32)
        aux = np.zeros((128, 4), np.float32)
        aux[:, 0] = np.tile(qs, 2)
        aux[:, 1] = np.tile(np.concatenate([qs[32:], qs[:32]]), 2)
        aux[:, 2] = np.tile(ks, 2)
        aux[:, 3] = np.tile(np.concatenate([ks[32:], ks[:32]]), 2)
        in_maps = []
        for c in range(NCORES):
            halo = (np.zeros((512, DM), np.float32) if c % 4 == 0
                    else t[c - 1][TOK - 512:])
            xin = np.concatenate([halo, t[c]], axis=0)
            in_maps.append({"x": np.ascontiguousarray(xin),
                            "wqkv": np.ascontiguousarray(inputs["L1_ffn_Wqkv"][i]).astype(np.float32, copy=False),
                            "wo": np.ascontiguousarray(inputs["L1_ffn_Wo"][i]).astype(np.float32, copy=False),
                            "aux": aux, "tabs": tabs[c], "mtiles": masks[c],
                            "pmrot": pm, "ind16": ind16, "indT": indT})
        r = run_bass_kernel_spmd(_PROG_CACHE["suba"], in_maps, core_ids=list(range(NCORES)))
        t = np.stack([np.asarray(r.results[c]["y"]) for c in range(NCORES)])
        W1 = np.asarray(inputs["L1_ffn_W1"][i], dtype=np.float32)
        b1 = np.asarray(inputs["L1_ffn_b1"][i], dtype=np.float32)
        w1p = np.zeros((DM, 44 * 128), np.float32)
        b1a = np.zeros((22, 128), np.float32)
        b1g = np.zeros((22, 128), np.float32)
        for j in range(22):
            jw = 42 if j == 21 else 128
            w1p[:, j * 256:j * 256 + jw] = W1[:, j * 128:j * 128 + jw]
            w1p[:, j * 256 + 128:j * 256 + 128 + jw] = W1[:, 2730 + j * 128:2730 + j * 128 + jw]
            b1a[j, :jw] = b1[j * 128:j * 128 + jw]
            b1g[j, :jw] = b1[2730 + j * 128:2730 + j * 128 + jw]
        vecs = np.concatenate([
            np.asarray(inputs["L1_ffn_ln_g"][i], np.float32).reshape(8, 128),
            np.asarray(inputs["L1_ffn_ln_b"][i], np.float32).reshape(8, 128),
            b1a, b1g,
            np.asarray(inputs["L1_ffn_b2"][i], np.float32).reshape(8, 128)], axis=0)
        in_maps = []
        for c in range(NCORES):
            halo = (np.zeros((128, DM), np.float32) if c % 4 == 0
                    else t[c - 1][TOK - 128:])
            xin = np.concatenate([halo, t[c]], axis=0)
            in_maps.append({"x": np.ascontiguousarray(xin),
                            "w1p": w1p,
                            "w2": np.ascontiguousarray(inputs["L1_ffn_W2"][i]).astype(np.float32, copy=False),
                            "vecs": vecs})
        r = run_bass_kernel_spmd(_PROG_CACHE["subf"], in_maps, core_ids=list(range(NCORES)))
        t = np.stack([np.asarray(r.results[c]["y"]) for c in range(NCORES)])
    in_maps = [{"x": xs[c], "t": np.ascontiguousarray(t[c])} for c in range(NCORES)]
    r = run_bass_kernel_spmd(_PROG_CACHE["postres"], in_maps, core_ids=list(range(NCORES)))
    out = np.stack([np.asarray(r.results[c]["y"]) for c in range(NCORES)])
    return out.reshape(2, 8192, DM)
